# version 19
# baseline (speedup 1.0000x reference)
import os
os.environ.setdefault("TRN_ALLOW_ICI_ALL_TO_ALL", "1")

import numpy as np
import concourse.bass as bass
import concourse.mybir as mybir
from concourse.bass import AP
from concourse.bass_utils import run_bass_kernel_spmd

F32 = mybir.dt.float32
BF16 = mybir.dt.bfloat16
AF = mybir.ActivationFunctionType
ALU = mybir.AluOpType
AX = mybir.AxisListType

B, S, D, NST, RK, HID = 2, 4096, 2048, 16, 64, 8192
DSL = 512
TEI = 512
S8192 = float(1.0 / np.sqrt(np.float64(8192.0)))
EPS = 1e-5
INV_D = float(1.0 / 2048.0)


def _hadamard(n):
    h = np.array([[1.0]], dtype=np.float32)
    while h.shape[0] < n:
        h = np.block([[h, h], [h, -h]]).astype(np.float32)
    return h


DBG = set()


def _tail(nc, ops, cnt, sems, block, finish):
    finish()
    streams = {"t": [], "s": [], "v": [], "g": [], "q": []}
    for e in ops:
        streams[e[0]].append(e)

    def make(eng_ops):
        def f(h):
            last = {}
            for _, fn, inc, waits in eng_ops:
                for sm, vl in waits:
                    if last.get(sm, 0) < vl:
                        h.wait_ge(sems[sm], vl)
                        last[sm] = vl
                ins = fn(h)
                if inc is not None:
                    ins.then_inc(sems[inc[0]], inc[1])
        return f

    block.tensor(make(streams["t"]))
    block.scalar(make(streams["s"]))
    block.vector(make(streams["v"]))
    block.gpsimd(make(streams["g"]))
    block.sync(make(streams["q"]))
    return nc


def build_nc(stop=99):
    from contextlib import ExitStack

    nc = bass.Bass("TRN2", target_bir_lowering=False)

    xt = nc.dram_tensor("xt", [128, 16, S], F32, kind="ExternalInput")
    xtsl = nc.dram_tensor("xtsl", [DSL, S], F32, kind="ExternalInput")
    xres = nc.dram_tensor("xres", [B, TEI, D], F32, kind="ExternalInput")
    wx = nc.dram_tensor("wx", [D, 96], F32, kind="ExternalInput")
    wdt = nc.dram_tensor("wdt", [RK, DSL], F32, kind="ExternalInput")
    bdt4 = nc.dram_tensor("bdt4", [4, 128], F32, kind="ExternalInput")
    alog4 = nc.dram_tensor("alog4", [4, 128, NST], F32, kind="ExternalInput")
    dp4 = nc.dram_tensor("dp4", [4, 128], F32, kind="ExternalInput")
    w1t = nc.dram_tensor("w1t", [128, 64, 16, 128], F32, kind="ExternalInput")
    w3t = nc.dram_tensor("w3t", [128, 64, 16, 128], F32, kind="ExternalInput")
    w2t = nc.dram_tensor("w2t", [128, 64, D], F32, kind="ExternalInput")
    hhm = nc.dram_tensor("hhm", [128, 128], F32, kind="ExternalInput")
    idm = nc.dram_tensor("idm", [128, 128], F32, kind="ExternalInput")
    out = nc.dram_tensor("out", [B, TEI, D], F32, kind="ExternalOutput")

    bcd = nc.dram_tensor("bcd", [32, S], BF16)
    barin = nc.dram_tensor("barin", [8, 64], F32)
    barout = nc.dram_tensor("barout", [8, 64], F32)
    a2aiW = [nc.dram_tensor(f"a2ai{w}", [512, DSL], BF16) for w in range(8)]
    a2aoW = [nc.dram_tensor(f"a2ao{w}", [512, DSL], BF16) for w in range(8)]
    z32d = nc.dram_tensor("z32d", [B, TEI, D], F32)
    if "dump" in DBG:
        dbgi = nc.dram_tensor("dbgi", [S, DSL], BF16, kind="ExternalOutput")
        dbgo = nc.dram_tensor("dbgo", [S, DSL], BF16, kind="ExternalOutput")
        dbgz = nc.dram_tensor("dbgz", [B, TEI, D], F32, kind="ExternalOutput")

    ops = []
    cnt = {k: 0 for k in "gqtsvc"}

    def op(eng, fn, inc=None, waits=()):
        ws = [(sm, vl) for (sm, vl) in waits if vl and vl > 0]
        for sm, vl in ws:
            assert vl <= cnt[sm], ("wait ahead of emission", sm, vl, cnt[sm])
        ops.append((eng, fn, inc, tuple(ws)))
        if inc is not None:
            cnt[inc[0]] += inc[1]
            return cnt[inc[0]]
        return None

    with ExitStack() as ctx:
        def sb(name, shape, dt):
            return ctx.enter_context(nc.sbuf_tensor(name, shape, dt))

        big0 = sb("big0", [128, 32768], BF16)
        big1 = sb("big1", [128, 32768], BF16)
        mid0 = sb("mid0", [128, 16384], BF16)
        fp0 = sb("fp0", [128, 6144], F32)
        wxb = sb("wxb", [128, 1536], BF16)
        wdtb = sb("wdtb", [RK, DSL], BF16)
        deltab = sb("deltab", [RK, 1024], BF16)
        bcrows = sb("bcrows", [32, S], BF16)
        identb = sb("identb", [128, 128], BF16)
        hhb = sb("hhb", [128, 128], BF16)
        alog_s = sb("alog_s", [128, 64], F32)
        negtmp = sb("negtmp", [128, 64], F32)
        bdt_s = sb("bdt_s", [128, 4], F32)
        dp_s = sb("dp_s", [128, 4], F32)
        carry = sb("carry", [128, 64], F32)
        ln_s = sb("ln_s", [128, 8], F32)
        eps_t = sb("eps_t", [128, 1], F32)
        vdum = sb("vdum", [128, 16], F32)

        PBt = ctx.enter_context(nc.psum_tensor("PB", [128, 4096], F32))
        PB = lambda a, b_: PBt[:, a:b_]

        sems = {k: ctx.enter_context(nc.semaphore("sem_" + k)) for k in "gqtsvc"}
        block = ctx.enter_context(nc.Block())

        dtb = lambda k: big0[:, 4096 * k: 4096 * (k + 1)]
        ub = lambda k: big0[:, 16384 + 4096 * k: 16384 + 4096 * (k + 1)]
        Bc = lambda n: big1[:, 1024 * n: 1024 * (n + 1)]
        Cc = lambda n: big1[:, 16384 + 1024 * n: 16384 + 1024 * (n + 1)]
        xtb = lambda i: mid0[:, 8192 * i: 8192 * (i + 1)]
        dtu2 = lambda i: mid0[:, 1024 * i: 1024 * (i + 1)]
        b_buf = mid0[:, 2048:3072]
        h_buf = mid0[:, 3072:4096]
        hCb = lambda i: mid0[:, 4096 + 1024 * i: 4096 + 1024 * (i + 1)]
        yDb = lambda i: mid0[:, 6144 + 1024 * i: 6144 + 1024 * (i + 1)]
        ytm = mid0[:, 8192:12288]
        a_rot = lambda i: fp0[:, 2048 * i: 2048 * i + 1024]
        spb = lambda i: fp0[:, 4096 + 512 * i: 4096 + 512 * (i + 1)]
        hidA = lambda i: big0[:, 512 * i: 512 * (i + 1)]
        w1b = lambda i: big1[:, 8192 + 2048 * i: 8192 + 2048 * (i + 1)]
        w3b = lambda i: big1[:, 12288 + 2048 * i: 12288 + 2048 * (i + 1)]
        silb = lambda i: big1[:, 16384 + 512 * i: 16384 + 512 * (i + 1)]
        gatb = lambda i: big1[:, 17408 + 512 * i: 17408 + 512 * (i + 1)]
        zbf = big1[:, 18432:20480]
        ytile = lambda i: big1[:, 20480 + 2048 * i: 20480 + 2048 * (i + 1)]
        accb = lambda ts: big1[:, 24576 + 2048 * ts: 24576 + 2048 * (ts + 1)]
        zTv = lambda k: mid0[:, 512 * k: 512 * (k + 1)]
        w2b = lambda i: mid0[:, 8192 * i: 8192 * (i + 1)]
        slotA = fp0[:, 0:2048]
        slotB = fp0[:, 2048:4096]
        slotC = fp0[:, 4096:6144]

        def bcast_src(row_ap):
            return AP(row_ap.tensor, row_ap.offset,
                      [list(row_ap.ap[0]), [0, 128], list(row_ap.ap[1])])

        op("g", lambda g: g.memset(eps_t[:, :], EPS), inc=("g", 16))
        op("g", lambda g: g.dma_start(
            out=wxb[:, :].rearrange("p (k r) -> p k r", k=16),
            in_=wx.ap().rearrange("(k p) r -> p k r", p=128)), inc=("g", 16))
        op("g", lambda g: g.dma_start(out=wdtb[:, :], in_=wdt.ap()), inc=("g", 16))
        op("g", lambda g: g.dma_start(out=identb[:, :], in_=idm.ap()), inc=("g", 16))
        op("g", lambda g: g.dma_start(out=hhb[:, :], in_=hhm.ap()), inc=("g", 16))
        ev_g_consts = cnt["g"]
        op("g", lambda g: g.dma_start(
            out=big0[:, 16384:32768].rearrange("p (k t) -> p k t", k=4),
            in_=xtsl.ap().rearrange("(k p) t -> p k t", p=128)), inc=("g", 16))
        ev_ub = cnt["g"]
        ev_alog = op("q", lambda q: q.dma_start(
            out=alog_s[:, :].rearrange("p (k n) -> p k n", k=4),
            in_=alog4.ap().rearrange("k p n -> p k n")),
            inc=("q", 16))
        op("q", lambda q: q.dma_start(
            out=bdt_s[:, :], in_=bdt4.ap().rearrange("k p -> p k"),
            allow_slow_non_contiguous=True), inc=("q", 16))
        op("q", lambda q: q.dma_start(
            out=dp_s[:, :], in_=dp4.ap().rearrange("k p -> p k"),
            allow_slow_non_contiguous=True), inc=("q", 16))
        ev_small_q = cnt["q"]
        ev_exp = op("s", lambda s: s.activation(negtmp[:, :], alog_s[:, :], AF.Exp),
                    waits=[("q", ev_alog)], inc=("s", 1))
        ev_mnega = op("v", lambda v: v.tensor_scalar_mul(alog_s[:, :], negtmp[:, :], -1.0),
                      waits=[("s", ev_exp)], inc=("v", 1))

        def _finish():
            tq = cnt["q"]
            tg = cnt["g"]
            op("q", lambda q: q.nop(), waits=[("q", tq)])
            op("g", lambda g: g.nop(), waits=[("g", tg)])

        dtu2w = lambda i: mid0[:, 512 * i: 512 * (i + 1)]
        b_w = mid0[:, 1024:1536]
        h_w = mid0[:, 1536:2048]
        hCw = lambda i: mid0[:, 2048 + 512 * i: 2048 + 512 * (i + 1)]
        yDw = lambda i: mid0[:, 3072 + 512 * i: 3072 + 512 * (i + 1)]
        ytmw = mid0[:, 4096:6144]
        xtbw = mid0[:, 8192:16384]
        Bcw = lambda n, par: big1[:, 8192 * par + 512 * n: 8192 * par + 512 * (n + 1)]
        Ccw = lambda n, par: big1[:, 16384 + 8192 * par + 512 * n:
                                  16384 + 8192 * par + 512 * (n + 1)]
        a_w = lambda i: fp0[:, 512 * i: 512 * (i + 1)]

        ev_xtb = [0] * 8
        ev_xz = [0] * 8
        ev_dtmm = [0] * 8
        ev_bc = [0] * 8
        ev_delta = [0] * 8
        ev_spfree = [[0] * 4 for _ in range(8)]
        ev_dtv = [[0] * 4 for _ in range(8)]
        ev_bcdw = [0] * 8
        ev_bcache = [0] * 8
        ev_a2aw = [0] * 8
        ev_hC_slot = [0, 0]
        ev_a_slot = [0, 0]
        ev_yD_slot = [0, 0]
        ev_ytr_slot = [0, 0]
        ev_py_free = [0, 0]
        ev_hC_last = [0] * 8
        ntr_ctr = [0]
        pyc = [0]

        def emit_1a(w):
            c0 = 512 * w
            wt = []
            if w >= 1:
                wt.append(("t", ev_xz[w - 1]))
            else:
                wt.append(("g", ev_g_consts))
            op("g", lambda g, c0=c0: g.dma_start(
                out=xtbw.rearrange("p (k t) -> p k t", k=16),
                in_=xt.ap()[:, :, c0:c0 + 512]), inc=("g", 16), waits=wt)
            ev_xtb[w] = cnt["g"]
            xzr = 512 * (w % 2)
            for k in range(16):
                wv = [("g", ev_xtb[w])]
                if k == 0 and w >= 2:
                    wv.append(("s", ev_bc[w - 2]))
                inc = ("t", 1) if k == 15 else None
                op("t", lambda t, k=k, xzr=xzr: t.matmul(
                    PB(xzr, xzr + 512)[0:96, :], wxb[:, 96 * k:96 * (k + 1)],
                    xtbw[:, 512 * k:512 * (k + 1)],
                    start=(k == 0), stop=(k == 15)), inc=inc, waits=wv)
            ev_xz[w] = cnt["t"]
            wv = [("t", ev_xz[w])]
            if w >= 2:
                wv.append(("t", ev_dtmm[w - 2]))
            op("s", lambda s, w=w, xzr=xzr: s.copy(
                deltab[:, 512 * (w % 2):512 * (w % 2) + 512],
                PB(xzr, xzr + 512)[0:64, :]), inc=("s", 1), waits=wv)
            ev_delta[w] = cnt["s"]
            op("s", lambda s, c0=c0, xzr=xzr: s.copy(
                bcrows[:, c0:c0 + 512], PB(xzr, xzr + 512)[64:96, :]), inc=("s", 1))
            ev_bc[w] = cnt["s"]
            for k in range(4):
                dtr = 1024 + 512 * (k % 2)
                wv = [("s", ev_delta[w])]
                if k > 1:
                    wv.append(("s", ev_spfree[w][k - 2]))
                elif w > 0:
                    wv.append(("s", ev_spfree[w - 1][2 + k]))
                ev_dt_k = op("t", lambda t, w=w, k=k, dtr=dtr: t.matmul(
                    PB(dtr, dtr + 512), wdtb[:, 128 * k:128 * (k + 1)],
                    deltab[:, 512 * (w % 2):512 * (w % 2) + 512],
                    start=True, stop=True), inc=("t", 1), waits=wv)
                if k == 3:
                    ev_dtmm[w] = ev_dt_k
                w2 = [("t", ev_dt_k)]
                if w == 0 and k == 0:
                    w2.append(("q", ev_small_q))
                if w > 0 or k > 0:
                    pj, pk = (w, k - 1) if k > 0 else (w - 1, 3)
                    w2.append(("v", ev_dtv[pj][pk]))
                op("s", lambda s, k=k, dtr=dtr: s.activation(
                    spb(0), PB(dtr, dtr + 512), AF.Abs, bias=bdt_s[:, k:k + 1]),
                    waits=w2)
                op("s", lambda s, k=k, dtr=dtr: s.activation(
                    spb(1), PB(dtr, dtr + 512), AF.Relu, bias=bdt_s[:, k:k + 1]),
                    inc=("s", 1))
                ev_spfree[w][k] = cnt["s"]
                op("s", lambda s: s.activation(spb(2), spb(0), AF.Exp, scale=-1.0))
                ev_spD = op("s", lambda s: s.activation(spb(0), spb(2), AF.Ln, bias=1.0),
                            inc=("s", 1))
                op("v", lambda v, k=k, c0=c0: v.tensor_tensor(
                    dtb(k)[:, c0:c0 + 512], spb(0), spb(1), ALU.add),
                    inc=("v", 1), waits=[("s", ev_spD)])
                ev_dtv[w][k] = cnt["v"]
            op("q", lambda q, c0=c0: q.dma_start(
                out=bcd.ap()[:, c0:c0 + 512], in_=bcrows[:, c0:c0 + 512]),
                inc=("q", 16), waits=[("s", ev_bc[w])])
            ev_bcdw[w] = cnt["q"]

        def emit_bcache(w):
            par = w % 2
            c0 = 512 * w
            wv = [("q", ev_bcdw[w])]
            if w >= 2:
                wv.append(("v", ev_hC_last[w - 2]))
            op("g", lambda g, par=par, c0=c0: g.dma_start(
                out=big1[:, 8192 * par: 8192 * par + 8192].rearrange(
                    "p (n t) -> p n t", n=16),
                in_=AP(bcd.ap().tensor, c0, [[0, 128], [S, 16], [1, 512]])),
                inc=("g", 16), waits=wv)
            op("g", lambda g, par=par, c0=c0: g.dma_start(
                out=big1[:, 16384 + 8192 * par: 16384 + 8192 * par + 8192].rearrange(
                    "p (n t) -> p n t", n=16),
                in_=AP(bcd.ap().tensor, 16 * S + c0, [[0, 128], [S, 16], [1, 512]])),
                inc=("g", 16))
            ev_bcache[w] = cnt["g"]

        def emit_1b(w):
            par = w % 2
            c0 = 512 * w
            for d in range(4):
                wv = [("g", ev_ub), ("v", ev_dtv[w][d])]
                op("v", lambda v, d=d, c0=c0: v.tensor_mul(
                    dtu2w(d % 2), dtb(d)[:, c0:c0 + 512], ub(d)[:, c0:c0 + 512]),
                    waits=wv)
                for n in range(16):
                    ia = n % 2
                    wv = [("v", max(ev_a_slot[ia], ev_mnega, ev_dtv[w][d]))]
                    ev_a = op("s", lambda s, d=d, n=n, c0=c0, ia=ia: s.activation(
                        a_w(ia), dtb(d)[:, c0:c0 + 512], AF.Exp,
                        scale=alog_s[:, 16 * d + n:16 * d + n + 1]),
                        inc=("s", 1), waits=wv)
                    op("v", lambda v, n=n, d=d, par=par: v.tensor_mul(
                        b_w, dtu2w(d % 2), Bcw(n, par)),
                        waits=[("g", ev_bcache[w])])
                    init = 0.0 if w == 0 else carry[:, 16 * d + n:16 * d + n + 1]
                    op("v", lambda v, ia=ia, init=init: v.tensor_tensor_scan(
                        h_w, a_w(ia), b_w, init, ALU.mult, ALU.add),
                        inc=("v", 1), waits=[("s", ev_a)])
                    ev_a_slot[ia] = cnt["v"]
                    ev_scan = cnt["v"]
                    if w < 7:
                        op("s", lambda s, d=d, n=n: s.copy(
                            carry[:, 16 * d + n:16 * d + n + 1], h_w[:, 511:512]),
                            waits=[("v", ev_scan)])
                    ic = n % 2
                    op("v", lambda v, n=n, ic=ic, par=par: v.tensor_mul(
                        hCw(ic), h_w, Ccw(n, par)), waits=[("t", ev_hC_slot[ic])])
                    ev_hC = op("v", lambda v: v.tensor_copy(
                        vdum[:, 8:16], vdum[:, 0:8]), inc=("v", 1))
                    pyr = 2048 + 512 * (pyc[0] % 2)
                    wv = [("v", ev_hC)]
                    if n == 0:
                        if w == 0 and d == 0:
                            wv.append(("s", S_P1A))
                        pv = ev_py_free[pyc[0] % 2]
                        if pv:
                            wv.append(("v", pv))
                    op("t", lambda t, pyr=pyr, ic=ic, n=n: t.matmul(
                        PB(pyr, pyr + 512), identb[:, :], hCw(ic),
                        start=(n == 0), stop=(n == 15)), inc=("t", 1), waits=wv)
                    ev_hC_slot[ic] = cnt["t"]
                ev_hC_last[w] = cnt["v"]
                ev_ymm = cnt["t"]
                iy = d % 2
                pyr = 2048 + 512 * (pyc[0] % 2)
                wv = [("t", ev_ymm), ("t", ev_yD_slot[iy]), ("q", ev_small_q)]
                op("v", lambda v, d=d, c0=c0, pyr=pyr, iy=iy:
                    v.scalar_tensor_tensor(
                        yDw(iy), ub(d)[:, c0:c0 + 512], dp_s[:, d:d + 1],
                        PB(pyr, pyr + 512), ALU.mult, ALU.add), waits=wv)
                ev_stt = op("v", lambda v: v.tensor_copy(
                    vdum[:, 8:16], vdum[:, 0:8]), inc=("v", 1))
                ev_py_free[pyc[0] % 2] = ev_stt
                pyc[0] += 1
                for ts in range(4):
                    m = ntr_ctr[0] % 2
                    ntr_ctr[0] += 1
                    wv = [("v", ev_stt), ("s", ev_ytr_slot[m])]
                    ev_ytr = op("t", lambda t, iy=iy, ts=ts, m=m: t.matmul(
                        PB(3072 + 512 * m, 3200 + 512 * m),
                        yDw(iy)[:, 128 * ts:128 * (ts + 1)], identb[:, :],
                        start=True, stop=True), inc=("t", 1), waits=wv)
                    ev_yD_slot[iy] = ev_ytr
                    wv = [("t", ev_ytr)]
                    if w >= 1 and d == 0 and ts == 0:
                        wv.append(("q", ev_a2aw[w - 1]))
                    op("s", lambda s, ts=ts, d=d, m=m: s.copy(
                        ytmw[:, 512 * ts + 128 * d: 512 * ts + 128 * (d + 1)],
                        PB(3072 + 512 * m, 3200 + 512 * m)), inc=("s", 1), waits=wv)
                    ev_ytr_slot[m] = cnt["s"]
            ev_ytm_done = cnt["s"]
            op("q", lambda q, w=w: q.dma_start(
                out=a2aiW[w].ap().rearrange("(ts p) d -> p ts d", p=128),
                in_=ytmw[:, :].rearrange("p (ts d) -> p ts d", ts=4)),
                inc=("q", 16), waits=[("s", ev_ytm_done)])
            ev_a2aw[w] = cnt["q"]
            op("g", lambda g, w=w: g.collective_compute(
                "AllToAll", ALU.bypass, replica_groups=[list(range(8))],
                ins=[a2aiW[w].ap().opt()], outs=[a2aoW[w].ap().opt()]),
                inc=("c", 1), waits=[("q", ev_a2aw[w])])

        S_P1A = 0
        for step in range(9):
            if step < 8:
                emit_1a(step)
                if step == 0:
                    S_P1A = cnt["s"]
                emit_bcache(step)
            if step >= 1:
                emit_1b(step - 1)
        S_P1B = cnt["s"]
        if stop <= 2:
            return _tail(nc, ops, cnt, sems, block, _finish)

        EV_CC = op("g", lambda g: g.collective_compute(
            "AllReduce", ALU.add, replica_groups=[list(range(8))],
            ins=[barin.ap().opt()], outs=[barout.ap().opt()]),
            inc=("c", 1))

        if stop <= 3:
            op("g", lambda g: g.nop(), waits=[("c", EV_CC)])
            return _tail(nc, ops, cnt, sems, block, _finish)
        if "dump" in DBG:
            op("q", lambda q: q.dma_start(out=dbgi.ap(), in_=a2ai.ap()),
               inc=("q", 16), waits=[("c", EV_CC)])
            op("q", lambda q: q.dma_start(out=dbgo.ap(), in_=a2ao.ap()),
               inc=("q", 16))
        lr_slotA_s = 0
        lr_slotB_s = 0
        lr_slotC_q = 0
        lr_ytile_v = [0, 0]
        lr_zbf_t = 0
        ztr_slot_s = [0] * 4
        ev_gatemm = {}
        ev_linmm = {}
        ev_silS = {}
        ev_gatV = {}
        ev_h128 = {}
        ev_hevac = {}
        ev_region_free = {}
        ev_down_last = 0
        ev_fold = 0
        ev_grp_end_v = 0
        for grp in range(2):
            ev_z32dma = [0] * 4
            ev_zT_written = 0
            for tt in range(4):
                it = tt % 2
                w = [("c", min(2 * tt + 2, EV_CC)), ("s", lr_slotA_s),
                     ("v", lr_ytile_v[it])]
                op("q", lambda q, grp=grp, tt=tt: q.dma_start(
                    out=slotA, in_=xres.ap()[grp, 128 * tt:128 * (tt + 1), :]),
                    inc=("q", 16), waits=w)
                for hw_ in range(2):
                    wdx = 2 * tt + hw_
                    for q2 in range(4):
                        r = 4 * grp + q2
                        op("q", lambda q, r=r, q2=q2, it=it, hw_=hw_, wdx=wdx:
                           q.dma_start(
                               out=ytile(it)[64 * hw_:64 * hw_ + 64,
                                             512 * q2:512 * (q2 + 1)],
                               in_=a2aoW[wdx].ap()[64 * r:64 * (r + 1), :]),
                           inc=("q", 16))
                ev_yt = cnt["q"]
                w = [("q", ev_yt), ("s", lr_slotB_s)]
                ev_resid = op("v", lambda v, it=it: v.tensor_tensor(
                    slotB, slotA, ytile(it), ALU.add), inc=("v", 1), waits=w)
                lr_ytile_v[it] = ev_resid
                ev_rs1 = op("v", lambda v: v.reduce_sum(
                    ln_s[:, 0:1], slotB, axis=AX.X), inc=("v", 1))
                ev_ng = op("s", lambda s: s.mul(ln_s[:, 1:2], ln_s[:, 0:1], -INV_D),
                           inc=("s", 1), waits=[("v", ev_rs1)])
                lr_slotB_s = op("s", lambda s: s.activation(
                    slotA, slotB, AF.Identity, bias=ln_s[:, 1:2]), inc=("s", 1),
                    waits=[("s", ev_ng)])
                ev_sq = op("s", lambda s: s.activation(slotC, slotA, AF.Square),
                           inc=("s", 1), waits=[("q", lr_slotC_q)])
                ev_rs2 = op("v", lambda v: v.reduce_sum(
                    ln_s[:, 2:3], slotC, axis=AX.X),
                    inc=("v", 1), waits=[("s", ev_sq)])
                op("s", lambda s: s.activation(
                    ln_s[:, 3:4], ln_s[:, 2:3], AF.Identity, bias=eps_t[:, :],
                    scale=INV_D), waits=[("v", ev_rs2), ("g", 16)])
                ev_sd = op("s", lambda s: s.activation(
                    ln_s[:, 4:5], ln_s[:, 3:4], AF.Sqrt), inc=("s", 1))
                ev_rstd = op("v", lambda v: v.reciprocal(ln_s[:, 5:6], ln_s[:, 4:5]),
                             inc=("v", 1), waits=[("s", ev_sd)])
                w = [("v", ev_rstd), ("t", lr_zbf_t)]
                ev_zbf = op("s", lambda s: s.activation(
                    zbf, slotA, AF.Copy, scale=ln_s[:, 5:6]), inc=("s", 1), waits=w)
                ev_z32 = op("s", lambda s: s.activation(
                    slotC, slotA, AF.Copy, scale=ln_s[:, 5:6]), inc=("s", 1))
                lr_slotA_s = ev_z32
                op("q", lambda q, grp=grp, tt=tt: q.dma_start(
                    out=z32d.ap()[grp, 128 * tt:128 * (tt + 1), :], in_=slotC),
                    inc=("q", 16), waits=[("s", ev_z32)])
                ev_z32dma[tt] = cnt["q"]
                lr_slotC_q = cnt["q"]
                for k in range(16):
                    m = k % 4
                    w = [("s", ev_zbf), ("s", ztr_slot_s[m])]
                    if grp == 0 and tt == 0 and k == 0:
                        w.append(("s", S_P1B))
                    if grp == 1 and tt == 0 and k < 4:
                        pv = ev_region_free.get(4 + m)
                        if pv:
                            w.append(("v", pv))
                    ev_ztr = op("t", lambda t, k=k, m=m: t.matmul(
                        PB(2048 + 512 * m, 2176 + 512 * m),
                        zbf[:, 128 * k:128 * (k + 1)], identb[:, :],
                        start=True, stop=True), inc=("t", 1), waits=w)
                    lr_zbf_t = ev_ztr
                    ztr_slot_s[m] = op("s", lambda s, k=k, tt=tt, m=m: s.copy(
                        zTv(k)[:, 128 * tt:128 * (tt + 1)],
                        PB(2048 + 512 * m, 2176 + 512 * m)),
                        inc=("s", 1), waits=[("t", ev_ztr)])
            ev_zT_written = cnt["s"]

            def emit_fold_blocks(i2):
                for st_ in range(6):
                    blksz = 2 << st_
                    if (i2 + 1) % blksz != 0:
                        break
                    m = i2 // blksz
                    inner = 512 << st_
                    base = m * blksz * 512
                    first_w = [("s", ev_hevac[64 * grp + i2])] if st_ == 0 else []
                    sub = min(inner, 8192)
                    for off in range(0, inner, sub):
                        A_ = big0[:, base + off: base + off + sub]
                        B_ = big0[:, base + inner + off: base + inner + off + sub]
                        S_ = big1[:, 0:sub]
                        op("v", lambda v, A_=A_, S_=S_: v.tensor_copy(S_, A_),
                           waits=first_w)
                        first_w = []
                        op("v", lambda v, A_=A_, B_=B_, S_=S_: v.tensor_tensor(
                            A_, S_, B_, ALU.add))
                        inc = ("v", 1) if (st_ == 5 and off + sub == inner) else None
                        op("v", lambda v, B_=B_, S_=S_: v.tensor_tensor(
                            B_, S_, B_, ALU.subtract), inc=inc)

            def emit_h128(iu2, i2):
                w2 = [("v", ev_gatV[iu2])]
                if iu2 >= 2:
                    w2.append(("s", ev_hevac[iu2 - 2]))
                ev_h128[iu2] = op("t", lambda t, i2=i2: t.matmul(
                    PB(1024 + 512 * (i2 % 2), 1536 + 512 * (i2 % 2)),
                    hhb[:, :], gatb(i2 % 2),
                    start=True, stop=True), inc=("t", 1), waits=w2)
                w2 = [("t", ev_h128[iu2])]
                if grp == 1 and i2 == 0:
                    w2.append(("t", ev_down_last))
                    w2.append(("v", ev_fold))
                ev_hevac[iu2] = op("s", lambda s, i2=i2: s.copy(
                    hidA(i2), PB(1024 + 512 * (i2 % 2), 1536 + 512 * (i2 % 2))),
                    inc=("s", 1), waits=w2)
                emit_fold_blocks(i2)

            for i in range(64):
                iu = 64 * grp + i
                w = []
                if iu >= 2:
                    w.append(("t", ev_gatemm[iu - 2]))
                else:
                    w.append(("v", ev_hC_last[7]))
                ev_w1 = op("g", lambda g, i=i: g.dma_start(
                    out=w1b(i % 2).rearrange("p (k h) -> p k h", k=16),
                    in_=w1t.ap()[:, i, :, :]), inc=("g", 16), waits=w)
                w = []
                if iu >= 2:
                    w.append(("t", ev_linmm[iu - 2]))
                else:
                    w.append(("v", ev_hC_last[7]))
                ev_w3 = op("g", lambda g, i=i: g.dma_start(
                    out=w3b(i % 2).rearrange("p (k h) -> p k h", k=16),
                    in_=w3t.ap()[:, i, :, :]), inc=("g", 16), waits=w)
                if i >= 1:
                    emit_h128(iu - 1, i - 1)
                gr = 2048 * (i % 2)
                lr = 512 + 2048 * (i % 2)
                for k in range(16):
                    w = [("g", ev_w1), ("s", ev_zT_written)]
                    if k == 0 and iu >= 2:
                        w.append(("s", ev_silS[iu - 2]))
                    inc = ("t", 1) if k == 15 else None
                    op("t", lambda t, i=i, k=k, gr=gr: t.matmul(
                        PB(gr, gr + 512), w1b(i % 2)[:, 128 * k:128 * (k + 1)], zTv(k),
                        start=(k == 0), stop=(k == 15)), inc=inc, waits=w)
                ev_gatemm[iu] = cnt["t"]
                for k in range(16):
                    w = [("g", ev_w3)]
                    if k == 0 and iu >= 2:
                        w.append(("v", ev_gatV[iu - 2]))
                    inc = ("t", 1) if k == 15 else None
                    op("t", lambda t, i=i, k=k, lr=lr: t.matmul(
                        PB(lr, lr + 512), w3b(i % 2)[:, 128 * k:128 * (k + 1)], zTv(k),
                        start=(k == 0), stop=(k == 15)), inc=inc, waits=w)
                ev_linmm[iu] = cnt["t"]
                w = [("t", ev_gatemm[iu])]
                if iu >= 2:
                    w.append(("v", ev_gatV[iu - 2]))
                ev_silS[iu] = op("s", lambda s, i=i, gr=gr: s.activation(
                    silb(i % 2), PB(gr, gr + 512), AF.Silu), inc=("s", 1), waits=w)
                w = [("s", ev_silS[iu]), ("t", ev_linmm[iu])]
                if iu >= 2:
                    w.append(("t", ev_h128[iu - 2]))
                op("v", lambda v, i=i, lr=lr: v.scalar_tensor_tensor(
                    gatb(i % 2), PB(lr, lr + 512), S8192, silb(i % 2),
                    ALU.mult, ALU.mult), waits=w)
                ev_gatV[iu] = op("v", lambda v: v.tensor_copy(
                    vdum[:, 8:16], vdum[:, 0:8]), inc=("v", 1))

                if i == 63:
                    emit_h128(iu, i)
            ev_up_done = cnt["s"]

            ev_fold = cnt["v"]

            ev_w2 = {}
            ev_dmm = {}
            ev_w2used = {}
            for sb_ in range(8):
                for hf in range(2):
                    w = [("v", ev_fold)]
                    if sb_ >= 1:
                        w.append(("t", ev_dmm[(sb_ - 1, 2 * 1 + 1)]))
                    ev_w2[(sb_, hf)] = op("g", lambda g, sb_=sb_, hf=hf: g.dma_start(
                        out=w2b(hf).rearrange("p (k d) -> p k d", k=4),
                        in_=w2t.ap()[:, 8 * sb_ + 4 * hf:8 * sb_ + 4 * hf + 4, :]),
                        inc=("g", 16), waits=w)
                for blk in range(2):
                    for hf in range(2):
                        for hk in range(4):
                            i = 8 * sb_ + 4 * hf + hk
                            st2 = (hf == 0 and hk == 0)
                            sp2 = (hf == 1 and hk == 3)
                            for ts in (2 * blk, 2 * blk + 1):
                                for ds in range(4):
                                    ridx = 4 * (ts % 2) + ds
                                    reg = 512 * ridx
                                    w = [("g", ev_w2[(sb_, hf)])]
                                    if st2 and ds == 0:
                                        w.append(("v", ev_fold))
                                    if st2:
                                        pv = ev_region_free.get(ridx)
                                        if pv:
                                            w.append(("v", pv))
                                        if grp == 0 and sb_ == 0 and blk == 0:
                                            w.append(("s", ev_zT_written))
                                    inc = ("t", 1) if (sp2 and ds == 3) else None
                                    op("t", lambda t, i=i, ts=ts, ds=ds, reg=reg,
                                        st2=st2, sp2=sp2, hf=hf, hk=hk: t.matmul(
                                        PB(reg, reg + 512),
                                        hidA(i)[:, 128 * ts:128 * ts + 128],
                                        w2b(hf)[:, 2048 * hk + 512 * ds:
                                                2048 * hk + 512 * ds + 512],
                                        start=st2, stop=sp2), inc=inc, waits=w)
                                    if sp2 and ds == 3:
                                        ev_dmm[(sb_, ts)] = cnt["t"]
                            if blk == 1 and hk == 3:
                                ev_w2used[(sb_, hf)] = cnt["t"]
                    for ts in (2 * blk, 2 * blk + 1):
                        for ds in range(4):
                            ridx = 4 * (ts % 2) + ds
                            reg = 512 * ridx
                            w = [("t", ev_dmm[(sb_, ts)])]
                            if sb_ == 0:
                                op("v", lambda v, ts=ts, ds=ds, reg=reg: v.tensor_copy(
                                    accb(ts)[:, 512 * ds:512 * (ds + 1)],
                                    PB(reg, reg + 512)), inc=("v", 1), waits=w)
                            else:
                                op("v", lambda v, ts=ts, ds=ds, reg=reg: v.tensor_tensor(
                                    accb(ts)[:, 512 * ds:512 * (ds + 1)],
                                    PB(reg, reg + 512),
                                    accb(ts)[:, 512 * ds:512 * (ds + 1)],
                                    ALU.add), inc=("v", 1), waits=w)
                            ev_region_free[ridx] = cnt["v"]
            ev_down_last = ev_dmm[(7, 3)]

            ev_outdma = 0
            ev_o_prev = 0
            for ts in range(4):
                w = [("q", ev_z32dma[ts]), ("s", max(ev_o_prev, lr_slotA_s))]
                op("q", lambda q, grp=grp, ts=ts: q.dma_start(
                    out=slotA, in_=z32d.ap()[grp, 128 * ts:128 * (ts + 1), :]),
                    inc=("q", 16), waits=w)
                ev_zr = cnt["q"]
                w = [("q", ev_zr), ("v", ev_region_free[4 * (ts % 2) + 3]),
                     ("s", lr_slotB_s)]
                ev_r2 = op("v", lambda v, ts=ts: v.tensor_tensor(
                    slotB, slotA, accb(ts), ALU.add), inc=("v", 1), waits=w)
                ev_rs1b = op("v", lambda v: v.reduce_sum(
                    ln_s[:, 0:1], slotB, axis=AX.X), inc=("v", 1))
                ev_ngb = op("s", lambda s: s.mul(ln_s[:, 1:2], ln_s[:, 0:1], -INV_D),
                            inc=("s", 1), waits=[("v", ev_rs1b)])
                lr_slotB_s = op("s", lambda s: s.activation(
                    slotA, slotB, AF.Identity, bias=ln_s[:, 1:2]), inc=("s", 1),
                    waits=[("s", ev_ngb)])
                ev_sq2 = op("s", lambda s: s.activation(slotC, slotA, AF.Square),
                            inc=("s", 1), waits=[("q", lr_slotC_q)])
                ev_rs2b = op("v", lambda v: v.reduce_sum(
                    ln_s[:, 2:3], slotC, axis=AX.X),
                    inc=("v", 1), waits=[("s", ev_sq2)])
                op("s", lambda s: s.activation(
                    ln_s[:, 3:4], ln_s[:, 2:3], AF.Identity, bias=eps_t[:, :],
                    scale=INV_D), waits=[("v", ev_rs2b), ("g", 16)])
                ev_sd2 = op("s", lambda s: s.activation(
                    ln_s[:, 4:5], ln_s[:, 3:4], AF.Sqrt), inc=("s", 1))
                ev_rstd2 = op("v", lambda v: v.reciprocal(ln_s[:, 5:6], ln_s[:, 4:5]),
                              inc=("v", 1), waits=[("s", ev_sd2)])
                ev_o = op("s", lambda s: s.activation(
                    slotC, slotA, AF.Copy, scale=ln_s[:, 5:6]),
                    inc=("s", 1), waits=[("v", ev_rstd2)])
                ev_o_prev = ev_o
                lr_slotA_s = ev_o
                op("q", lambda q, grp=grp, ts=ts: q.dma_start(
                    out=out.ap()[grp, 128 * ts:128 * (ts + 1), :], in_=slotC),
                    inc=("q", 16), waits=[("s", ev_o)])
                ev_outdma = cnt["q"]
                lr_slotC_q = ev_outdma
            ev_grp_end_v = cnt["v"]

        if "dump" in DBG:
            vq = cnt["q"]
            op("q", lambda q: q.dma_start(out=dbgz.ap(), in_=z32d.ap()),
               inc=("q", 16), waits=[("q", vq)])
        total_q = cnt["q"]
        op("q", lambda q: q.nop(), waits=[("q", total_q)])

        streams = {"t": [], "s": [], "v": [], "g": [], "q": []}
        for e in ops:
            streams[e[0]].append(e)

        def make(eng_ops):
            def f(h):
                last = {}
                for _, fn, inc, waits in eng_ops:
                    for sm, vl in waits:
                        if last.get(sm, 0) < vl:
                            h.wait_ge(sems[sm], vl)
                            last[sm] = vl
                    ins = fn(h)
                    if inc is not None:
                        ins.then_inc(sems[inc[0]], inc[1])
            return f

        block.tensor(make(streams["t"]))
        block.scalar(make(streams["s"]))
        block.vector(make(streams["v"]))
        block.gpsimd(make(streams["g"]))
        block.sync(make(streams["q"]))

    return nc


def kernel(L0_attn_x, L0_attn_Wx, L0_attn_Wdt, L0_attn_bdt, L0_attn_A_log,
           L0_attn_Dp, L0_ffn_W1, L0_ffn_W2, L0_ffn_W3):
    x = np.ascontiguousarray(np.asarray(L0_attn_x, dtype=np.float32))
    Wx = np.ascontiguousarray(np.asarray(L0_attn_Wx, dtype=np.float32))
    Wdt = np.asarray(L0_attn_Wdt, dtype=np.float32)
    bdt = np.asarray(L0_attn_bdt, dtype=np.float32)
    A_log = np.asarray(L0_attn_A_log, dtype=np.float32)
    Dp = np.asarray(L0_attn_Dp, dtype=np.float32)
    w1t = np.ascontiguousarray(np.asarray(L0_ffn_W1, dtype=np.float32).T)
    w2t = np.ascontiguousarray(np.asarray(L0_ffn_W2, dtype=np.float32).T)
    w3t = np.ascontiguousarray(np.asarray(L0_ffn_W3, dtype=np.float32).T)
    hh = _hadamard(128)
    idm = np.eye(128, dtype=np.float32)

    xT = [np.ascontiguousarray(x[b].T) for b in range(B)]
    xts = [np.ascontiguousarray(xT[b].reshape(16, 128, S).transpose(1, 0, 2))
           for b in range(B)]
    w1s = np.ascontiguousarray(w1t.reshape(16, 128, 64, 128).transpose(1, 2, 0, 3))
    w3s = np.ascontiguousarray(w3t.reshape(16, 128, 64, 128).transpose(1, 2, 0, 3))
    w2s = np.ascontiguousarray(w2t.reshape(64, 128, D).transpose(1, 0, 2))

    in_maps = []
    tokmap = [np.concatenate([np.arange(512 * w + 64 * c, 512 * w + 64 * (c + 1))
                              for w in range(8)]) for c in range(8)]
    for c in range(8):
        b, sl = c // 4, c % 4
        dlo = DSL * sl
        in_maps.append({
            "xt": xts[b],
            "xtsl": np.ascontiguousarray(xT[b][dlo:dlo + DSL]),
            "xres": np.ascontiguousarray(x[:, tokmap[c], :]),
            "wx": Wx,
            "wdt": np.ascontiguousarray(Wdt[:, dlo:dlo + DSL]),
            "bdt4": np.ascontiguousarray(bdt[dlo:dlo + DSL].reshape(4, 128)),
            "alog4": np.ascontiguousarray(A_log[dlo:dlo + DSL].reshape(4, 128, NST)),
            "dp4": np.ascontiguousarray(Dp[dlo:dlo + DSL].reshape(4, 128)),
            "w1t": w1s, "w3t": w3s, "w2t": w2s,
            "hhm": hh, "idm": idm,
        })

    nc = build_nc()
    res = run_bass_kernel_spmd(nc, in_maps, core_ids=list(range(8)))
    out_full = np.empty((B, S, D), dtype=np.float32)
    for c in range(8):
        out_full[:, tokmap[c], :] = res.results[c]["out"]
    return out_full


# version 20
# speedup vs baseline: 1.0414x; 1.0414x over previous
import os
os.environ.setdefault("TRN_ALLOW_ICI_ALL_TO_ALL", "1")

import numpy as np
import concourse.bass as bass
import concourse.mybir as mybir
from concourse.bass import AP
from concourse.bass_utils import run_bass_kernel_spmd

F32 = mybir.dt.float32
BF16 = mybir.dt.bfloat16
AF = mybir.ActivationFunctionType
ALU = mybir.AluOpType
AX = mybir.AxisListType

B, S, D, NST, RK, HID = 2, 4096, 2048, 16, 64, 8192
DSL = 512
TEI = 512
S8192 = float(1.0 / np.sqrt(np.float64(8192.0)))
EPS = 1e-5
INV_D = float(1.0 / 2048.0)


def _hadamard(n):
    h = np.array([[1.0]], dtype=np.float32)
    while h.shape[0] < n:
        h = np.block([[h, h], [h, -h]]).astype(np.float32)
    return h


DBG = set()


def _tail(nc, ops, cnt, sems, block, finish):
    finish()
    streams = {"t": [], "s": [], "v": [], "g": [], "q": []}
    for e in ops:
        streams[e[0]].append(e)

    def make(eng_ops):
        def f(h):
            last = {}
            for _, fn, inc, waits in eng_ops:
                for sm, vl in waits:
                    if last.get(sm, 0) < vl:
                        h.wait_ge(sems[sm], vl)
                        last[sm] = vl
                ins = fn(h)
                if inc is not None:
                    ins.then_inc(sems[inc[0]], inc[1])
        return f

    block.tensor(make(streams["t"]))
    block.scalar(make(streams["s"]))
    block.vector(make(streams["v"]))
    block.gpsimd(make(streams["g"]))
    block.sync(make(streams["q"]))
    return nc


def build_nc(stop=99):
    from contextlib import ExitStack

    nc = bass.Bass("TRN2", target_bir_lowering=False)

    xt = nc.dram_tensor("xt", [128, 16, S], F32, kind="ExternalInput")
    xtsl = nc.dram_tensor("xtsl", [DSL, S], F32, kind="ExternalInput")
    xres = nc.dram_tensor("xres", [B, TEI, D], F32, kind="ExternalInput")
    wx = nc.dram_tensor("wx", [D, 96], F32, kind="ExternalInput")
    wdt = nc.dram_tensor("wdt", [RK, DSL], F32, kind="ExternalInput")
    bdt4 = nc.dram_tensor("bdt4", [4, 128], F32, kind="ExternalInput")
    alog4 = nc.dram_tensor("alog4", [4, 128, NST], F32, kind="ExternalInput")
    dp4 = nc.dram_tensor("dp4", [4, 128], F32, kind="ExternalInput")
    w1t = nc.dram_tensor("w1t", [128, 64, 16, 128], F32, kind="ExternalInput")
    w3t = nc.dram_tensor("w3t", [128, 64, 16, 128], F32, kind="ExternalInput")
    w2t = nc.dram_tensor("w2t", [128, 64, D], F32, kind="ExternalInput")
    hhm = nc.dram_tensor("hhm", [128, 128], F32, kind="ExternalInput")
    idm = nc.dram_tensor("idm", [128, 128], F32, kind="ExternalInput")
    out = nc.dram_tensor("out", [B, TEI, D], F32, kind="ExternalOutput")

    bcd = nc.dram_tensor("bcd", [32, S], BF16)
    barin = nc.dram_tensor("barin", [8, 64], F32)
    barout = nc.dram_tensor("barout", [8, 64], F32)
    a2aiW = [nc.dram_tensor(f"a2ai{w}", [512, DSL], BF16) for w in range(8)]
    a2aoW = [nc.dram_tensor(f"a2ao{w}", [512, DSL], BF16) for w in range(8)]
    z32d = nc.dram_tensor("z32d", [B, TEI, D], F32)
    if "dump" in DBG:
        dbgi = nc.dram_tensor("dbgi", [S, DSL], BF16, kind="ExternalOutput")
        dbgo = nc.dram_tensor("dbgo", [S, DSL], BF16, kind="ExternalOutput")
        dbgz = nc.dram_tensor("dbgz", [B, TEI, D], F32, kind="ExternalOutput")

    ops = []
    cnt = {k: 0 for k in "gqtsvc"}

    def op(eng, fn, inc=None, waits=()):
        ws = [(sm, vl) for (sm, vl) in waits if vl and vl > 0]
        for sm, vl in ws:
            assert vl <= cnt[sm], ("wait ahead of emission", sm, vl, cnt[sm])
        ops.append((eng, fn, inc, tuple(ws)))
        if inc is not None:
            cnt[inc[0]] += inc[1]
            return cnt[inc[0]]
        return None

    with ExitStack() as ctx:
        def sb(name, shape, dt):
            return ctx.enter_context(nc.sbuf_tensor(name, shape, dt))

        big0 = sb("big0", [128, 32768], BF16)
        big1 = sb("big1", [128, 32768], BF16)
        mid0 = sb("mid0", [128, 16384], BF16)
        fp0 = sb("fp0", [128, 6144], F32)
        wxb = sb("wxb", [128, 1536], BF16)
        wdtb = sb("wdtb", [RK, DSL], BF16)
        deltab = sb("deltab", [RK, 1024], BF16)
        bcrows = sb("bcrows", [32, S], BF16)
        identb = sb("identb", [128, 128], BF16)
        hhb = sb("hhb", [128, 128], BF16)
        alog_s = sb("alog_s", [128, 64], F32)
        negtmp = sb("negtmp", [128, 64], F32)
        bdt_s = sb("bdt_s", [128, 4], F32)
        dp_s = sb("dp_s", [128, 4], F32)
        carry = sb("carry", [128, 64], F32)
        ln_s = sb("ln_s", [128, 8], F32)
        eps_t = sb("eps_t", [128, 1], F32)
        vdum = sb("vdum", [128, 16], F32)

        PBt = ctx.enter_context(nc.psum_tensor("PB", [128, 4096], F32))
        PB = lambda a, b_: PBt[:, a:b_]

        sems = {k: ctx.enter_context(nc.semaphore("sem_" + k)) for k in "gqtsvc"}
        block = ctx.enter_context(nc.Block())

        dtb = lambda k: big0[:, 4096 * k: 4096 * (k + 1)]
        ub = lambda k: big0[:, 16384 + 4096 * k: 16384 + 4096 * (k + 1)]
        Bc = lambda n: big1[:, 1024 * n: 1024 * (n + 1)]
        Cc = lambda n: big1[:, 16384 + 1024 * n: 16384 + 1024 * (n + 1)]
        xtb = lambda i: mid0[:, 8192 * i: 8192 * (i + 1)]
        dtu2 = lambda i: mid0[:, 1024 * i: 1024 * (i + 1)]
        b_buf = mid0[:, 2048:3072]
        h_buf = mid0[:, 3072:4096]
        hCb = lambda i: mid0[:, 4096 + 1024 * i: 4096 + 1024 * (i + 1)]
        yDb = lambda i: mid0[:, 6144 + 1024 * i: 6144 + 1024 * (i + 1)]
        ytm = mid0[:, 8192:12288]
        a_rot = lambda i: fp0[:, 2048 * i: 2048 * i + 1024]
        spb = lambda i: fp0[:, 4096 + 512 * i: 4096 + 512 * (i + 1)]
        hidA = lambda i: big0[:, 512 * i: 512 * (i + 1)]
        w1b = lambda i: big1[:, 8192 + 2048 * i: 8192 + 2048 * (i + 1)]
        w3b = lambda i: big1[:, 12288 + 2048 * i: 12288 + 2048 * (i + 1)]
        silb = lambda i: big1[:, 16384 + 512 * i: 16384 + 512 * (i + 1)]
        gatb = lambda i: big1[:, 17408 + 512 * i: 17408 + 512 * (i + 1)]
        zbf = big1[:, 18432:20480]
        ytile = lambda i: big1[:, 20480 + 2048 * i: 20480 + 2048 * (i + 1)]
        accb = lambda ts: big1[:, 24576 + 2048 * ts: 24576 + 2048 * (ts + 1)]
        zTv = lambda k: mid0[:, 512 * k: 512 * (k + 1)]
        w2b = lambda i: mid0[:, 8192 * i: 8192 * (i + 1)]
        slotA = fp0[:, 0:2048]
        slotB = fp0[:, 2048:4096]
        slotC = fp0[:, 4096:6144]

        def bcast_src(row_ap):
            return AP(row_ap.tensor, row_ap.offset,
                      [list(row_ap.ap[0]), [0, 128], list(row_ap.ap[1])])

        op("g", lambda g: g.memset(eps_t[:, :], EPS), inc=("g", 16))
        op("g", lambda g: g.dma_start(
            out=wxb[:, :].rearrange("p (k r) -> p k r", k=16),
            in_=wx.ap().rearrange("(k p) r -> p k r", p=128)), inc=("g", 16))
        op("g", lambda g: g.dma_start(out=wdtb[:, :], in_=wdt.ap()), inc=("g", 16))
        op("g", lambda g: g.dma_start(out=identb[:, :], in_=idm.ap()), inc=("g", 16))
        op("g", lambda g: g.dma_start(out=hhb[:, :], in_=hhm.ap()), inc=("g", 16))
        ev_g_consts = cnt["g"]
        op("g", lambda g: g.dma_start(
            out=big0[:, 16384:32768].rearrange("p (k t) -> p k t", k=4),
            in_=xtsl.ap().rearrange("(k p) t -> p k t", p=128)), inc=("g", 16))
        ev_ub = cnt["g"]
        ev_alog = op("q", lambda q: q.dma_start(
            out=alog_s[:, :].rearrange("p (k n) -> p k n", k=4),
            in_=alog4.ap().rearrange("k p n -> p k n")),
            inc=("q", 16))
        op("q", lambda q: q.dma_start(
            out=bdt_s[:, :], in_=bdt4.ap().rearrange("k p -> p k"),
            allow_slow_non_contiguous=True), inc=("q", 16))
        op("q", lambda q: q.dma_start(
            out=dp_s[:, :], in_=dp4.ap().rearrange("k p -> p k"),
            allow_slow_non_contiguous=True), inc=("q", 16))
        ev_small_q = cnt["q"]
        ev_exp = op("s", lambda s: s.activation(negtmp[:, :], alog_s[:, :], AF.Exp),
                    waits=[("q", ev_alog)], inc=("s", 1))
        ev_mnega = op("v", lambda v: v.tensor_scalar_mul(alog_s[:, :], negtmp[:, :], -1.0),
                      waits=[("s", ev_exp)], inc=("v", 1))

        def _finish():
            tq = cnt["q"]
            tg = cnt["g"]
            op("q", lambda q: q.nop(), waits=[("q", tq)])
            op("g", lambda g: g.nop(), waits=[("g", tg)])

        dtu2w = lambda i: mid0[:, 512 * i: 512 * (i + 1)]
        b_w = mid0[:, 1024:1536]
        h_w = mid0[:, 1536:2048]
        hCw = lambda i: mid0[:, 2048 + 512 * i: 2048 + 512 * (i + 1)]
        yDw = lambda i: mid0[:, 3072 + 512 * i: 3072 + 512 * (i + 1)]
        ytmw = mid0[:, 4096:6144]
        xtbw = mid0[:, 8192:16384]
        Bcw = lambda n, par: big1[:, 8192 * par + 512 * n: 8192 * par + 512 * (n + 1)]
        Ccw = lambda n, par: big1[:, 16384 + 8192 * par + 512 * n:
                                  16384 + 8192 * par + 512 * (n + 1)]
        a_w = lambda i: fp0[:, 512 * i: 512 * (i + 1)]

        ev_xtb = [0] * 8
        ev_xz = [0] * 8
        ev_dtmm = [0] * 8
        ev_bc = [0] * 8
        ev_delta = [0] * 8
        ev_spfree = [[0] * 4 for _ in range(8)]
        ev_dtv = [[0] * 4 for _ in range(8)]
        ev_bcdw = [0] * 8
        ev_bcache = [0] * 8
        ev_a2aw = [0] * 8
        ev_hC_slot = [0, 0]
        ev_a_slot = [0, 0]
        ev_yD_slot = [0, 0]
        ev_ytr_slot = [0, 0]
        ev_py_free = [0, 0]
        ev_hC_last = [0] * 8
        ntr_ctr = [0]
        pyc = [0]

        def emit_1a(w):
            c0 = 512 * w
            wt = []
            if w >= 1:
                wt.append(("t", ev_xz[w - 1]))
            else:
                wt.append(("g", ev_g_consts))
            op("g", lambda g, c0=c0: g.dma_start(
                out=xtbw.rearrange("p (k t) -> p k t", k=16),
                in_=xt.ap()[:, :, c0:c0 + 512]), inc=("g", 16), waits=wt)
            ev_xtb[w] = cnt["g"]
            xzr = 512 * (w % 2)
            for k in range(16):
                wv = [("g", ev_xtb[w])]
                if k == 0 and w >= 2:
                    wv.append(("s", ev_bc[w - 2]))
                inc = ("t", 1) if k == 15 else None
                op("t", lambda t, k=k, xzr=xzr: t.matmul(
                    PB(xzr, xzr + 512)[0:96, :], wxb[:, 96 * k:96 * (k + 1)],
                    xtbw[:, 512 * k:512 * (k + 1)],
                    start=(k == 0), stop=(k == 15)), inc=inc, waits=wv)
            ev_xz[w] = cnt["t"]
            wv = [("t", ev_xz[w])]
            if w >= 2:
                wv.append(("t", ev_dtmm[w - 2]))
            op("s", lambda s, w=w, xzr=xzr: s.copy(
                deltab[:, 512 * (w % 2):512 * (w % 2) + 512],
                PB(xzr, xzr + 512)[0:64, :]), inc=("s", 1), waits=wv)
            ev_delta[w] = cnt["s"]
            op("s", lambda s, c0=c0, xzr=xzr: s.copy(
                bcrows[:, c0:c0 + 512], PB(xzr, xzr + 512)[64:96, :]), inc=("s", 1))
            ev_bc[w] = cnt["s"]
            for k in range(4):
                dtr = 1024 + 512 * (k % 2)
                wv = [("s", ev_delta[w])]
                if k > 1:
                    wv.append(("s", ev_spfree[w][k - 2]))
                elif w > 0:
                    wv.append(("s", ev_spfree[w - 1][2 + k]))
                ev_dt_k = op("t", lambda t, w=w, k=k, dtr=dtr: t.matmul(
                    PB(dtr, dtr + 512), wdtb[:, 128 * k:128 * (k + 1)],
                    deltab[:, 512 * (w % 2):512 * (w % 2) + 512],
                    start=True, stop=True), inc=("t", 1), waits=wv)
                if k == 3:
                    ev_dtmm[w] = ev_dt_k
                w2 = [("t", ev_dt_k)]
                if w == 0 and k == 0:
                    w2.append(("q", ev_small_q))
                if w > 0 or k > 0:
                    pj, pk = (w, k - 1) if k > 0 else (w - 1, 3)
                    w2.append(("v", ev_dtv[pj][pk]))
                op("s", lambda s, k=k, dtr=dtr: s.activation(
                    spb(0), PB(dtr, dtr + 512), AF.Abs, bias=bdt_s[:, k:k + 1]),
                    waits=w2)
                op("s", lambda s, k=k, dtr=dtr: s.activation(
                    spb(1), PB(dtr, dtr + 512), AF.Relu, bias=bdt_s[:, k:k + 1]),
                    inc=("s", 1))
                ev_spfree[w][k] = cnt["s"]
                op("s", lambda s: s.activation(spb(2), spb(0), AF.Exp, scale=-1.0))
                ev_spD = op("s", lambda s: s.activation(spb(0), spb(2), AF.Ln, bias=1.0),
                            inc=("s", 1))
                op("v", lambda v, k=k, c0=c0: v.tensor_tensor(
                    dtb(k)[:, c0:c0 + 512], spb(0), spb(1), ALU.add),
                    inc=("v", 1), waits=[("s", ev_spD)])
                ev_dtv[w][k] = cnt["v"]
            op("q", lambda q, c0=c0: q.dma_start(
                out=bcd.ap()[:, c0:c0 + 512], in_=bcrows[:, c0:c0 + 512]),
                inc=("q", 16), waits=[("s", ev_bc[w])])
            ev_bcdw[w] = cnt["q"]

        def emit_bcache(w):
            par = w % 2
            c0 = 512 * w
            wv = [("q", ev_bcdw[w])]
            if w >= 2:
                wv.append(("v", ev_hC_last[w - 2]))
            op("g", lambda g, par=par, c0=c0: g.dma_start(
                out=big1[:, 8192 * par: 8192 * par + 8192].rearrange(
                    "p (n t) -> p n t", n=16),
                in_=AP(bcd.ap().tensor, c0, [[0, 128], [S, 16], [1, 512]])),
                inc=("g", 16), waits=wv)
            op("g", lambda g, par=par, c0=c0: g.dma_start(
                out=big1[:, 16384 + 8192 * par: 16384 + 8192 * par + 8192].rearrange(
                    "p (n t) -> p n t", n=16),
                in_=AP(bcd.ap().tensor, 16 * S + c0, [[0, 128], [S, 16], [1, 512]])),
                inc=("g", 16))
            ev_bcache[w] = cnt["g"]

        def emit_1b(w):
            par = w % 2
            c0 = 512 * w
            for d in range(4):
                wv = [("g", ev_ub), ("v", ev_dtv[w][d])]
                op("v", lambda v, d=d, c0=c0: v.tensor_mul(
                    dtu2w(d % 2), dtb(d)[:, c0:c0 + 512], ub(d)[:, c0:c0 + 512]),
                    waits=wv)
                for n in range(16):
                    ia = n % 2
                    wv = [("v", max(ev_a_slot[ia], ev_mnega, ev_dtv[w][d]))]
                    ev_a = op("s", lambda s, d=d, n=n, c0=c0, ia=ia: s.activation(
                        a_w(ia), dtb(d)[:, c0:c0 + 512], AF.Exp,
                        scale=alog_s[:, 16 * d + n:16 * d + n + 1]),
                        inc=("s", 1), waits=wv)
                    op("v", lambda v, n=n, d=d, par=par: v.tensor_mul(
                        b_w, dtu2w(d % 2), Bcw(n, par)),
                        waits=[("g", ev_bcache[w])])
                    init = 0.0 if w == 0 else carry[:, 16 * d + n:16 * d + n + 1]
                    op("v", lambda v, ia=ia, init=init: v.tensor_tensor_scan(
                        h_w, a_w(ia), b_w, init, ALU.mult, ALU.add),
                        inc=("v", 1), waits=[("s", ev_a)])
                    ev_a_slot[ia] = cnt["v"]
                    ev_scan = cnt["v"]
                    if w < 7:
                        op("s", lambda s, d=d, n=n: s.copy(
                            carry[:, 16 * d + n:16 * d + n + 1], h_w[:, 511:512]),
                            waits=[("v", ev_scan)])
                    ic = n % 2
                    op("v", lambda v, n=n, ic=ic, par=par: v.tensor_mul(
                        hCw(ic), h_w, Ccw(n, par)), waits=[("t", ev_hC_slot[ic])])
                    ev_hC = op("v", lambda v: v.tensor_copy(
                        vdum[:, 8:16], vdum[:, 0:8]), inc=("v", 1))
                    pyr = 2048 + 512 * (pyc[0] % 2)
                    wv = [("v", ev_hC)]
                    if n == 0:
                        if w == 0 and d == 0:
                            wv.append(("s", S_P1A))
                        pv = ev_py_free[pyc[0] % 2]
                        if pv:
                            wv.append(("v", pv))
                    op("t", lambda t, pyr=pyr, ic=ic, n=n: t.matmul(
                        PB(pyr, pyr + 512), identb[:, :], hCw(ic),
                        start=(n == 0), stop=(n == 15)), inc=("t", 1), waits=wv)
                    ev_hC_slot[ic] = cnt["t"]
                ev_hC_last[w] = cnt["v"]
                ev_ymm = cnt["t"]
                iy = d % 2
                pyr = 2048 + 512 * (pyc[0] % 2)
                wv = [("t", ev_ymm), ("t", ev_yD_slot[iy]), ("q", ev_small_q)]
                op("v", lambda v, d=d, c0=c0, pyr=pyr, iy=iy:
                    v.scalar_tensor_tensor(
                        yDw(iy), ub(d)[:, c0:c0 + 512], dp_s[:, d:d + 1],
                        PB(pyr, pyr + 512), ALU.mult, ALU.add), waits=wv)
                ev_stt = op("v", lambda v: v.tensor_copy(
                    vdum[:, 8:16], vdum[:, 0:8]), inc=("v", 1))
                ev_py_free[pyc[0] % 2] = ev_stt
                pyc[0] += 1
                for ts in range(4):
                    m = ntr_ctr[0] % 2
                    ntr_ctr[0] += 1
                    wv = [("v", ev_stt), ("s", ev_ytr_slot[m])]
                    ev_ytr = op("t", lambda t, iy=iy, ts=ts, m=m: t.matmul(
                        PB(3072 + 512 * m, 3200 + 512 * m),
                        yDw(iy)[:, 128 * ts:128 * (ts + 1)], identb[:, :],
                        start=True, stop=True), inc=("t", 1), waits=wv)
                    ev_yD_slot[iy] = ev_ytr
                    wv = [("t", ev_ytr)]
                    if w >= 1 and d == 0 and ts == 0:
                        wv.append(("q", ev_a2aw[w - 1]))
                    op("s", lambda s, ts=ts, d=d, m=m: s.copy(
                        ytmw[:, 512 * ts + 128 * d: 512 * ts + 128 * (d + 1)],
                        PB(3072 + 512 * m, 3200 + 512 * m)), inc=("s", 1), waits=wv)
                    ev_ytr_slot[m] = cnt["s"]
            ev_ytm_done = cnt["s"]
            op("q", lambda q, w=w: q.dma_start(
                out=a2aiW[w].ap().rearrange("(ts p) d -> p ts d", p=128),
                in_=ytmw[:, :].rearrange("p (ts d) -> p ts d", ts=4)),
                inc=("q", 16), waits=[("s", ev_ytm_done)])
            ev_a2aw[w] = cnt["q"]
            op("g", lambda g, w=w: g.collective_compute(
                "AllToAll", ALU.bypass, replica_groups=[list(range(8))],
                ins=[a2aiW[w].ap().opt()], outs=[a2aoW[w].ap().opt()]),
                inc=("c", 1), waits=[("q", ev_a2aw[w])])

        S_P1A = 0
        for step in range(9):
            if step < 8:
                emit_1a(step)
                if step == 0:
                    S_P1A = cnt["s"]
                emit_bcache(step)
            if step >= 1:
                emit_1b(step - 1)
        S_P1B = cnt["s"]
        if stop <= 2:
            return _tail(nc, ops, cnt, sems, block, _finish)

        EV_CC = op("g", lambda g: g.collective_compute(
            "AllReduce", ALU.add, replica_groups=[list(range(8))],
            ins=[barin.ap().opt()], outs=[barout.ap().opt()]),
            inc=("c", 1))

        if stop <= 3:
            op("g", lambda g: g.nop(), waits=[("c", EV_CC)])
            return _tail(nc, ops, cnt, sems, block, _finish)
        if "dump" in DBG:
            op("q", lambda q: q.dma_start(out=dbgi.ap(), in_=a2ai.ap()),
               inc=("q", 16), waits=[("c", EV_CC)])
            op("q", lambda q: q.dma_start(out=dbgo.ap(), in_=a2ao.ap()),
               inc=("q", 16))
        lr_slotA_s = 0
        lr_slotB_s = 0
        lr_slotC_q = 0
        lr_ytile_v = [0, 0]
        lr_zbf_t = 0
        ztr_slot_s = [0] * 4
        ev_gatemm = {}
        ev_linmm = {}
        ev_silS = {}
        ev_gatV = {}
        ev_h128 = {}
        ev_hevac = {}
        ev_region_free = {}
        ev_down_last = 0
        ev_fold = 0
        ev_grp_end_v = 0
        for grp in range(2):
            ev_z32dma = [0] * 4
            ev_zT_written = 0
            for tt in range(4):
                it = tt % 2
                w = [("c", min(2 * tt + 2, EV_CC)), ("s", lr_slotA_s),
                     ("v", lr_ytile_v[it])]
                op("q", lambda q, grp=grp, tt=tt: q.dma_start(
                    out=slotA, in_=xres.ap()[grp, 128 * tt:128 * (tt + 1), :]),
                    inc=("q", 16), waits=w)
                for hw_ in range(2):
                    wdx = 2 * tt + hw_
                    for q2 in range(4):
                        r = 4 * grp + q2
                        op("q", lambda q, r=r, q2=q2, it=it, hw_=hw_, wdx=wdx:
                           q.dma_start(
                               out=ytile(it)[64 * hw_:64 * hw_ + 64,
                                             512 * q2:512 * (q2 + 1)],
                               in_=a2aoW[wdx].ap()[64 * r:64 * (r + 1), :]),
                           inc=("q", 16))
                ev_yt = cnt["q"]
                w = [("q", ev_yt), ("s", lr_slotB_s)]
                ev_resid = op("v", lambda v, it=it: v.tensor_tensor(
                    slotB, slotA, ytile(it), ALU.add), inc=("v", 1), waits=w)
                lr_ytile_v[it] = ev_resid
                ev_rs1 = op("v", lambda v: v.reduce_sum(
                    ln_s[:, 0:1], slotB, axis=AX.X), inc=("v", 1))
                ev_ng = op("s", lambda s: s.mul(ln_s[:, 1:2], ln_s[:, 0:1], -INV_D),
                           inc=("s", 1), waits=[("v", ev_rs1)])
                lr_slotB_s = op("s", lambda s: s.activation(
                    slotA, slotB, AF.Identity, bias=ln_s[:, 1:2]), inc=("s", 1),
                    waits=[("s", ev_ng)])
                ev_sq = op("s", lambda s: s.activation(slotC, slotA, AF.Square),
                           inc=("s", 1), waits=[("q", lr_slotC_q)])
                ev_rs2 = op("v", lambda v: v.reduce_sum(
                    ln_s[:, 2:3], slotC, axis=AX.X),
                    inc=("v", 1), waits=[("s", ev_sq)])
                op("s", lambda s: s.activation(
                    ln_s[:, 3:4], ln_s[:, 2:3], AF.Identity, bias=eps_t[:, :],
                    scale=INV_D), waits=[("v", ev_rs2), ("g", 16)])
                ev_sd = op("s", lambda s: s.activation(
                    ln_s[:, 4:5], ln_s[:, 3:4], AF.Sqrt), inc=("s", 1))
                ev_rstd = op("v", lambda v: v.reciprocal(ln_s[:, 5:6], ln_s[:, 4:5]),
                             inc=("v", 1), waits=[("s", ev_sd)])
                w = [("v", ev_rstd), ("t", lr_zbf_t)]
                ev_zbf = op("s", lambda s: s.activation(
                    zbf, slotA, AF.Copy, scale=ln_s[:, 5:6]), inc=("s", 1), waits=w)
                ev_z32 = op("s", lambda s: s.activation(
                    slotC, slotA, AF.Copy, scale=ln_s[:, 5:6]), inc=("s", 1))
                lr_slotA_s = ev_z32
                op("q", lambda q, grp=grp, tt=tt: q.dma_start(
                    out=z32d.ap()[grp, 128 * tt:128 * (tt + 1), :], in_=slotC),
                    inc=("q", 16), waits=[("s", ev_z32)])
                ev_z32dma[tt] = cnt["q"]
                lr_slotC_q = cnt["q"]
                for k in range(16):
                    m = k % 4
                    w = [("s", ev_zbf), ("s", ztr_slot_s[m])]
                    if grp == 0 and tt == 0 and k == 0:
                        w.append(("s", S_P1B))
                    if grp == 1 and tt == 0 and k < 4:
                        pv = ev_region_free.get(4 + m)
                        if pv:
                            w.append(("v", pv))
                    ev_ztr = op("t", lambda t, k=k, m=m: t.matmul(
                        PB(2048 + 512 * m, 2176 + 512 * m),
                        zbf[:, 128 * k:128 * (k + 1)], identb[:, :],
                        start=True, stop=True), inc=("t", 1), waits=w)
                    lr_zbf_t = ev_ztr
                    ztr_slot_s[m] = op("s", lambda s, k=k, tt=tt, m=m: s.copy(
                        zTv(k)[:, 128 * tt:128 * (tt + 1)],
                        PB(2048 + 512 * m, 2176 + 512 * m)),
                        inc=("s", 1), waits=[("t", ev_ztr)])
            ev_zT_written = cnt["s"]

            def emit_fold_blocks(i2):
                for st_ in range(6):
                    blksz = 2 << st_
                    if (i2 + 1) % blksz != 0:
                        break
                    m = i2 // blksz
                    inner = 512 << st_
                    base = m * blksz * 512
                    first_w = [("s", ev_hevac[64 * grp + i2])] if st_ == 0 else []
                    sub = min(inner, 8192)
                    for off in range(0, inner, sub):
                        A_ = big0[:, base + off: base + off + sub]
                        B_ = big0[:, base + inner + off: base + inner + off + sub]
                        S_ = big1[:, 0:sub]
                        op("v", lambda v, A_=A_, S_=S_: v.tensor_copy(S_, A_),
                           waits=first_w)
                        first_w = []
                        op("v", lambda v, A_=A_, B_=B_, S_=S_: v.tensor_tensor(
                            A_, S_, B_, ALU.add))
                        inc = ("v", 1) if (st_ == 5 and off + sub == inner) else None
                        op("v", lambda v, B_=B_, S_=S_: v.tensor_tensor(
                            B_, S_, B_, ALU.subtract), inc=inc)

            def emit_h128(iu2, i2):
                w2 = [("v", ev_gatV[iu2])]
                if iu2 >= 2:
                    w2.append(("s", ev_hevac[iu2 - 2]))
                ev_h128[iu2] = op("t", lambda t, i2=i2: t.matmul(
                    PB(1024 + 512 * (i2 % 2), 1536 + 512 * (i2 % 2)),
                    hhb[:, :], gatb(i2 % 2),
                    start=True, stop=True), inc=("t", 1), waits=w2)
                w2 = [("t", ev_h128[iu2])]
                if grp == 1 and i2 == 0:
                    w2.append(("t", ev_down_last))
                    w2.append(("v", ev_fold))
                ev_hevac[iu2] = op("s", lambda s, i2=i2: s.copy(
                    hidA(i2), PB(1024 + 512 * (i2 % 2), 1536 + 512 * (i2 % 2))),
                    inc=("s", 1), waits=w2)
                emit_fold_blocks(i2)

            for i in range(64):
                iu = 64 * grp + i
                w = []
                if iu >= 2:
                    w.append(("t", ev_gatemm[iu - 2]))
                else:
                    w.append(("v", ev_hC_last[7]))
                ev_w1 = op("g", lambda g, i=i: g.dma_start(
                    out=w1b(i % 2).rearrange("p (k h) -> p k h", k=16),
                    in_=w1t.ap()[:, i, :, :]), inc=("g", 16), waits=w)
                w = []
                if iu >= 2:
                    w.append(("t", ev_linmm[iu - 2]))
                else:
                    w.append(("v", ev_hC_last[7]))
                ev_w3 = op("g", lambda g, i=i: g.dma_start(
                    out=w3b(i % 2).rearrange("p (k h) -> p k h", k=16),
                    in_=w3t.ap()[:, i, :, :]), inc=("g", 16), waits=w)
                if i >= 1:
                    emit_h128(iu - 1, i - 1)
                gr = 2048 * (i % 2)
                lr = 512 + 2048 * (i % 2)
                for k in range(16):
                    w = [("g", ev_w1), ("s", ev_zT_written)]
                    if k == 0 and iu >= 2:
                        w.append(("s", ev_silS[iu - 2]))
                    inc = ("t", 1) if k == 15 else None
                    op("t", lambda t, i=i, k=k, gr=gr: t.matmul(
                        PB(gr, gr + 512), w1b(i % 2)[:, 128 * k:128 * (k + 1)], zTv(k),
                        start=(k == 0), stop=(k == 15)), inc=inc, waits=w)
                ev_gatemm[iu] = cnt["t"]
                for k in range(16):
                    w = [("g", ev_w3)]
                    if k == 0 and iu >= 2:
                        w.append(("v", ev_gatV[iu - 2]))
                    inc = ("t", 1) if k == 15 else None
                    op("t", lambda t, i=i, k=k, lr=lr: t.matmul(
                        PB(lr, lr + 512), w3b(i % 2)[:, 128 * k:128 * (k + 1)], zTv(k),
                        start=(k == 0), stop=(k == 15)), inc=inc, waits=w)
                ev_linmm[iu] = cnt["t"]
                w = [("t", ev_gatemm[iu])]
                if iu >= 2:
                    w.append(("v", ev_gatV[iu - 2]))
                ev_silS[iu] = op("s", lambda s, i=i, gr=gr: s.activation(
                    silb(i % 2), PB(gr, gr + 512), AF.Silu), inc=("s", 1), waits=w)
                w = [("s", ev_silS[iu]), ("t", ev_linmm[iu])]
                if iu >= 2:
                    w.append(("t", ev_h128[iu - 2]))
                op("v", lambda v, i=i, lr=lr: v.scalar_tensor_tensor(
                    gatb(i % 2), PB(lr, lr + 512), S8192, silb(i % 2),
                    ALU.mult, ALU.mult), waits=w)
                ev_gatV[iu] = op("v", lambda v: v.tensor_copy(
                    vdum[:, 8:16], vdum[:, 0:8]), inc=("v", 1))

                if i == 63:
                    emit_h128(iu, i)
            ev_up_done = cnt["s"]

            ev_fold = cnt["v"]

            ev_w2 = {}
            ev_dmm = {}
            ev_w2used = {}
            for sb_ in range(8):
                for hf in range(2):
                    w = []
                    if sb_ == 0 and hf == 1:
                        w.append(("t", ev_xz[7]))
                        w.append(("q", ev_a2aw[7]))
                    elif sb_ == 0:
                        w.append(("t", ev_linmm[64 * grp + 63]))
                        w.append(("q", ev_a2aw[7]))
                    else:
                        w.append(("t", ev_w2used[(sb_ - 1, hf)]))
                    ev_w2[(sb_, hf)] = op("g", lambda g, sb_=sb_, hf=hf: g.dma_start(
                        out=w2b(hf).rearrange("p (k d) -> p k d", k=4),
                        in_=w2t.ap()[:, 8 * sb_ + 4 * hf:8 * sb_ + 4 * hf + 4, :]),
                        inc=("g", 16), waits=w)
                for blk in range(2):
                    for hf in range(2):
                        for hk in range(4):
                            i = 8 * sb_ + 4 * hf + hk
                            st2 = (hf == 0 and hk == 0)
                            sp2 = (hf == 1 and hk == 3)
                            for ts in (2 * blk, 2 * blk + 1):
                                for ds in range(4):
                                    ridx = 4 * (ts % 2) + ds
                                    reg = 512 * ridx
                                    w = [("g", ev_w2[(sb_, hf)])]
                                    if st2 and ds == 0:
                                        w.append(("v", ev_fold))
                                    if st2:
                                        pv = ev_region_free.get(ridx)
                                        if pv:
                                            w.append(("v", pv))
                                        if grp == 0 and sb_ == 0 and blk == 0:
                                            w.append(("s", ev_zT_written))
                                    last_of_hfblk = (hk == 3 and ds == 3
                                                     and ts == 2 * blk + 1)
                                    inc = ("t", 1) if ((sp2 and ds == 3)
                                                       or (blk == 1 and last_of_hfblk)) else None
                                    op("t", lambda t, i=i, ts=ts, ds=ds, reg=reg,
                                        st2=st2, sp2=sp2, hf=hf, hk=hk: t.matmul(
                                        PB(reg, reg + 512),
                                        hidA(i)[:, 128 * ts:128 * ts + 128],
                                        w2b(hf)[:, 2048 * hk + 512 * ds:
                                                2048 * hk + 512 * ds + 512],
                                        start=st2, stop=sp2), inc=inc, waits=w)
                                    if sp2 and ds == 3:
                                        ev_dmm[(sb_, ts)] = cnt["t"]
                                    if blk == 1 and last_of_hfblk:
                                        ev_w2used[(sb_, hf)] = cnt["t"]
                    for ts in (2 * blk, 2 * blk + 1):
                        for ds in range(4):
                            ridx = 4 * (ts % 2) + ds
                            reg = 512 * ridx
                            w = [("t", ev_dmm[(sb_, ts)])]
                            if sb_ == 0:
                                op("v", lambda v, ts=ts, ds=ds, reg=reg: v.tensor_copy(
                                    accb(ts)[:, 512 * ds:512 * (ds + 1)],
                                    PB(reg, reg + 512)), inc=("v", 1), waits=w)
                            else:
                                op("v", lambda v, ts=ts, ds=ds, reg=reg: v.tensor_tensor(
                                    accb(ts)[:, 512 * ds:512 * (ds + 1)],
                                    PB(reg, reg + 512),
                                    accb(ts)[:, 512 * ds:512 * (ds + 1)],
                                    ALU.add), inc=("v", 1), waits=w)
                            ev_region_free[ridx] = cnt["v"]
            ev_down_last = ev_dmm[(7, 3)]

            ev_outdma = 0
            ev_o_prev = 0
            for ts in range(4):
                w = [("q", ev_z32dma[ts]), ("s", max(ev_o_prev, lr_slotA_s))]
                op("q", lambda q, grp=grp, ts=ts: q.dma_start(
                    out=slotA, in_=z32d.ap()[grp, 128 * ts:128 * (ts + 1), :]),
                    inc=("q", 16), waits=w)
                ev_zr = cnt["q"]
                w = [("q", ev_zr), ("v", ev_region_free[4 * (ts % 2) + 3]),
                     ("s", lr_slotB_s)]
                ev_r2 = op("v", lambda v, ts=ts: v.tensor_tensor(
                    slotB, slotA, accb(ts), ALU.add), inc=("v", 1), waits=w)
                ev_rs1b = op("v", lambda v: v.reduce_sum(
                    ln_s[:, 0:1], slotB, axis=AX.X), inc=("v", 1))
                ev_ngb = op("s", lambda s: s.mul(ln_s[:, 1:2], ln_s[:, 0:1], -INV_D),
                            inc=("s", 1), waits=[("v", ev_rs1b)])
                lr_slotB_s = op("s", lambda s: s.activation(
                    slotA, slotB, AF.Identity, bias=ln_s[:, 1:2]), inc=("s", 1),
                    waits=[("s", ev_ngb)])
                ev_sq2 = op("s", lambda s: s.activation(slotC, slotA, AF.Square),
                            inc=("s", 1), waits=[("q", lr_slotC_q)])
                ev_rs2b = op("v", lambda v: v.reduce_sum(
                    ln_s[:, 2:3], slotC, axis=AX.X),
                    inc=("v", 1), waits=[("s", ev_sq2)])
                op("s", lambda s: s.activation(
                    ln_s[:, 3:4], ln_s[:, 2:3], AF.Identity, bias=eps_t[:, :],
                    scale=INV_D), waits=[("v", ev_rs2b), ("g", 16)])
                ev_sd2 = op("s", lambda s: s.activation(
                    ln_s[:, 4:5], ln_s[:, 3:4], AF.Sqrt), inc=("s", 1))
                ev_rstd2 = op("v", lambda v: v.reciprocal(ln_s[:, 5:6], ln_s[:, 4:5]),
                              inc=("v", 1), waits=[("s", ev_sd2)])
                ev_o = op("s", lambda s: s.activation(
                    slotC, slotA, AF.Copy, scale=ln_s[:, 5:6]),
                    inc=("s", 1), waits=[("v", ev_rstd2)])
                ev_o_prev = ev_o
                lr_slotA_s = ev_o
                op("q", lambda q, grp=grp, ts=ts: q.dma_start(
                    out=out.ap()[grp, 128 * ts:128 * (ts + 1), :], in_=slotC),
                    inc=("q", 16), waits=[("s", ev_o)])
                ev_outdma = cnt["q"]
                lr_slotC_q = ev_outdma
            ev_grp_end_v = cnt["v"]

        if "dump" in DBG:
            vq = cnt["q"]
            op("q", lambda q: q.dma_start(out=dbgz.ap(), in_=z32d.ap()),
               inc=("q", 16), waits=[("q", vq)])
        total_q = cnt["q"]
        op("q", lambda q: q.nop(), waits=[("q", total_q)])

        streams = {"t": [], "s": [], "v": [], "g": [], "q": []}
        for e in ops:
            streams[e[0]].append(e)

        def make(eng_ops):
            def f(h):
                last = {}
                for _, fn, inc, waits in eng_ops:
                    for sm, vl in waits:
                        if last.get(sm, 0) < vl:
                            h.wait_ge(sems[sm], vl)
                            last[sm] = vl
                    ins = fn(h)
                    if inc is not None:
                        ins.then_inc(sems[inc[0]], inc[1])
            return f

        block.tensor(make(streams["t"]))
        block.scalar(make(streams["s"]))
        block.vector(make(streams["v"]))
        block.gpsimd(make(streams["g"]))
        block.sync(make(streams["q"]))

    return nc


def kernel(L0_attn_x, L0_attn_Wx, L0_attn_Wdt, L0_attn_bdt, L0_attn_A_log,
           L0_attn_Dp, L0_ffn_W1, L0_ffn_W2, L0_ffn_W3):
    x = np.ascontiguousarray(np.asarray(L0_attn_x, dtype=np.float32))
    Wx = np.ascontiguousarray(np.asarray(L0_attn_Wx, dtype=np.float32))
    Wdt = np.asarray(L0_attn_Wdt, dtype=np.float32)
    bdt = np.asarray(L0_attn_bdt, dtype=np.float32)
    A_log = np.asarray(L0_attn_A_log, dtype=np.float32)
    Dp = np.asarray(L0_attn_Dp, dtype=np.float32)
    w1t = np.ascontiguousarray(np.asarray(L0_ffn_W1, dtype=np.float32).T)
    w2t = np.ascontiguousarray(np.asarray(L0_ffn_W2, dtype=np.float32).T)
    w3t = np.ascontiguousarray(np.asarray(L0_ffn_W3, dtype=np.float32).T)
    hh = _hadamard(128)
    idm = np.eye(128, dtype=np.float32)

    xT = [np.ascontiguousarray(x[b].T) for b in range(B)]
    xts = [np.ascontiguousarray(xT[b].reshape(16, 128, S).transpose(1, 0, 2))
           for b in range(B)]
    w1s = np.ascontiguousarray(w1t.reshape(16, 128, 64, 128).transpose(1, 2, 0, 3))
    w3s = np.ascontiguousarray(w3t.reshape(16, 128, 64, 128).transpose(1, 2, 0, 3))
    w2s = np.ascontiguousarray(w2t.reshape(64, 128, D).transpose(1, 0, 2))

    in_maps = []
    tokmap = [np.concatenate([np.arange(512 * w + 64 * c, 512 * w + 64 * (c + 1))
                              for w in range(8)]) for c in range(8)]
    for c in range(8):
        b, sl = c // 4, c % 4
        dlo = DSL * sl
        in_maps.append({
            "xt": xts[b],
            "xtsl": np.ascontiguousarray(xT[b][dlo:dlo + DSL]),
            "xres": np.ascontiguousarray(x[:, tokmap[c], :]),
            "wx": Wx,
            "wdt": np.ascontiguousarray(Wdt[:, dlo:dlo + DSL]),
            "bdt4": np.ascontiguousarray(bdt[dlo:dlo + DSL].reshape(4, 128)),
            "alog4": np.ascontiguousarray(A_log[dlo:dlo + DSL].reshape(4, 128, NST)),
            "dp4": np.ascontiguousarray(Dp[dlo:dlo + DSL].reshape(4, 128)),
            "w1t": w1s, "w3t": w3s, "w2t": w2s,
            "hhm": hh, "idm": idm,
        })

    nc = build_nc()
    res = run_bass_kernel_spmd(nc, in_maps, core_ids=list(range(8)))
    out_full = np.empty((B, S, D), dtype=np.float32)
    for c in range(8):
        out_full[:, tokmap[c], :] = res.results[c]["out"]
    return out_full


# version 22
# speedup vs baseline: 1.0767x; 1.0339x over previous
import os
os.environ.setdefault("TRN_ALLOW_ICI_ALL_TO_ALL", "1")

import numpy as np
import concourse.bass as bass
import concourse.mybir as mybir
from concourse.bass import AP
from concourse.bass_utils import run_bass_kernel_spmd

F32 = mybir.dt.float32
BF16 = mybir.dt.bfloat16
AF = mybir.ActivationFunctionType
ALU = mybir.AluOpType
AX = mybir.AxisListType

B, S, D, NST, RK, HID = 2, 4096, 2048, 16, 64, 8192
DSL = 512
TEI = 512
S8192 = float(1.0 / np.sqrt(np.float64(8192.0)))
EPS = 1e-5
INV_D = float(1.0 / 2048.0)


def _hadamard(n):
    h = np.array([[1.0]], dtype=np.float32)
    while h.shape[0] < n:
        h = np.block([[h, h], [h, -h]]).astype(np.float32)
    return h


DBG = set()


def _tail(nc, ops, cnt, sems, block, finish):
    finish()
    streams = {"t": [], "s": [], "v": [], "g": [], "q": []}
    for e in ops:
        streams[e[0]].append(e)

    def make(eng_ops):
        def f(h):
            last = {}
            for _, fn, inc, waits in eng_ops:
                for sm, vl in waits:
                    if last.get(sm, 0) < vl:
                        h.wait_ge(sems[sm], vl)
                        last[sm] = vl
                ins = fn(h)
                if inc is not None:
                    ins.then_inc(sems[inc[0]], inc[1])
        return f

    block.tensor(make(streams["t"]))
    block.scalar(make(streams["s"]))
    block.vector(make(streams["v"]))
    block.gpsimd(make(streams["g"]))
    block.sync(make(streams["q"]))
    return nc


def build_nc(stop=99):
    from contextlib import ExitStack

    nc = bass.Bass("TRN2", target_bir_lowering=False)

    xt = nc.dram_tensor("xt", [128, 16, S], F32, kind="ExternalInput")
    xtsl = nc.dram_tensor("xtsl", [DSL, S], F32, kind="ExternalInput")
    xres = nc.dram_tensor("xres", [B, TEI, D], F32, kind="ExternalInput")
    wx = nc.dram_tensor("wx", [D, 96], F32, kind="ExternalInput")
    wdt = nc.dram_tensor("wdt", [RK, DSL], F32, kind="ExternalInput")
    bdt4 = nc.dram_tensor("bdt4", [4, 128], F32, kind="ExternalInput")
    alog4 = nc.dram_tensor("alog4", [4, 128, NST], F32, kind="ExternalInput")
    dp4 = nc.dram_tensor("dp4", [4, 128], F32, kind="ExternalInput")
    w1t = nc.dram_tensor("w1t", [128, 64, 16, 128], F32, kind="ExternalInput")
    w3t = nc.dram_tensor("w3t", [128, 64, 16, 128], F32, kind="ExternalInput")
    w2t = nc.dram_tensor("w2t", [128, 64, D], F32, kind="ExternalInput")
    hhm = nc.dram_tensor("hhm", [128, 128], F32, kind="ExternalInput")
    idm = nc.dram_tensor("idm", [128, 128], F32, kind="ExternalInput")
    out = nc.dram_tensor("out", [B, TEI, D], F32, kind="ExternalOutput")

    bcdB = nc.dram_tensor("bcdB", [8, 16, 512], BF16)
    bcdC = nc.dram_tensor("bcdC", [8, 16, 512], BF16)
    barin = nc.dram_tensor("barin", [8, 64], F32)
    barout = nc.dram_tensor("barout", [8, 64], F32)
    a2ai = nc.dram_tensor("a2ai", [S, DSL], BF16)
    a2ao = nc.dram_tensor("a2ao", [S, DSL], BF16)
    z32d = nc.dram_tensor("z32d", [B, TEI, D], F32)
    if "dump" in DBG:
        dbgi = nc.dram_tensor("dbgi", [S, DSL], BF16, kind="ExternalOutput")
        dbgo = nc.dram_tensor("dbgo", [S, DSL], BF16, kind="ExternalOutput")
        dbgz = nc.dram_tensor("dbgz", [B, TEI, D], F32, kind="ExternalOutput")

    ops = []
    cnt = {k: 0 for k in "gqtsvc"}

    def op(eng, fn, inc=None, waits=()):
        ws = [(sm, vl) for (sm, vl) in waits if vl and vl > 0]
        for sm, vl in ws:
            assert vl <= cnt[sm], ("wait ahead of emission", sm, vl, cnt[sm])
        ops.append((eng, fn, inc, tuple(ws)))
        if inc is not None:
            cnt[inc[0]] += inc[1]
            return cnt[inc[0]]
        return None

    with ExitStack() as ctx:
        def sb(name, shape, dt):
            return ctx.enter_context(nc.sbuf_tensor(name, shape, dt))

        big0 = sb("big0", [128, 32768], BF16)
        big1 = sb("big1", [128, 32768], BF16)
        mid0 = sb("mid0", [128, 16384], BF16)
        fp0 = sb("fp0", [128, 6144], F32)
        wxb = sb("wxb", [128, 1536], BF16)
        wdtb = sb("wdtb", [RK, DSL], BF16)
        deltab = sb("deltab", [RK, 1024], BF16)
        bcrows = sb("bcrows", [32, S], BF16)
        identb = sb("identb", [128, 128], BF16)
        hhb = sb("hhb", [128, 128], BF16)
        alog_s = sb("alog_s", [128, 64], F32)
        negtmp = sb("negtmp", [128, 64], F32)
        bdt_s = sb("bdt_s", [128, 4], F32)
        dp_s = sb("dp_s", [128, 4], F32)
        carry = sb("carry", [128, 64], F32)
        ln_s = sb("ln_s", [128, 8], F32)
        eps_t = sb("eps_t", [128, 1], F32)
        vdum = sb("vdum", [128, 16], F32)

        PBt = ctx.enter_context(nc.psum_tensor("PB", [128, 4096], F32))
        PB = lambda a, b_: PBt[:, a:b_]

        sems = {k: ctx.enter_context(nc.semaphore("sem_" + k)) for k in "gqtsvc"}
        block = ctx.enter_context(nc.Block())

        dtb = lambda k: big0[:, 4096 * k: 4096 * (k + 1)]
        ub = lambda k: big0[:, 16384 + 4096 * k: 16384 + 4096 * (k + 1)]
        Bc = lambda n: big1[:, 1024 * n: 1024 * (n + 1)]
        Cc = lambda n: big1[:, 16384 + 1024 * n: 16384 + 1024 * (n + 1)]
        xtb = lambda i: mid0[:, 8192 * i: 8192 * (i + 1)]
        dtu2 = lambda i: mid0[:, 1024 * i: 1024 * (i + 1)]
        b_buf = mid0[:, 2048:3072]
        h_buf = mid0[:, 3072:4096]
        hCb = lambda i: mid0[:, 4096 + 1024 * i: 4096 + 1024 * (i + 1)]
        yDb = lambda i: mid0[:, 6144 + 1024 * i: 6144 + 1024 * (i + 1)]
        ytm = mid0[:, 8192:12288]
        a_rot = lambda i: fp0[:, 2048 * i: 2048 * i + 1024]
        spb = lambda i: fp0[:, 4096 + 512 * i: 4096 + 512 * (i + 1)]
        hidA = lambda i: big0[:, 512 * i: 512 * (i + 1)]
        w1b = lambda i: big1[:, 8192 + 2048 * i: 8192 + 2048 * (i + 1)]
        w3b = lambda i: big1[:, 12288 + 2048 * i: 12288 + 2048 * (i + 1)]
        silb = lambda i: big1[:, 16384 + 512 * i: 16384 + 512 * (i + 1)]
        gatb = lambda i: big1[:, 17408 + 512 * i: 17408 + 512 * (i + 1)]
        zbf = big1[:, 18432:20480]
        ytile = lambda i: big1[:, 20480 + 2048 * i: 20480 + 2048 * (i + 1)]
        accb = lambda ts: big1[:, 24576 + 2048 * ts: 24576 + 2048 * (ts + 1)]
        zTv = lambda k: mid0[:, 512 * k: 512 * (k + 1)]
        w2b = lambda i: mid0[:, 8192 * i: 8192 * (i + 1)]
        slotA = fp0[:, 0:2048]
        slotB = fp0[:, 2048:4096]
        slotC = fp0[:, 4096:6144]

        def bcast_src(row_ap):
            return AP(row_ap.tensor, row_ap.offset,
                      [list(row_ap.ap[0]), [0, 128], list(row_ap.ap[1])])

        op("g", lambda g: g.memset(eps_t[:, :], EPS), inc=("g", 16))
        op("g", lambda g: g.dma_start(
            out=wxb[:, :].rearrange("p (k r) -> p k r", k=16),
            in_=wx.ap().rearrange("(k p) r -> p k r", p=128)), inc=("g", 16))
        op("g", lambda g: g.dma_start(out=wdtb[:, :], in_=wdt.ap()), inc=("g", 16))
        op("g", lambda g: g.dma_start(out=identb[:, :], in_=idm.ap()), inc=("g", 16))
        op("g", lambda g: g.dma_start(out=hhb[:, :], in_=hhm.ap()), inc=("g", 16))
        ev_g_consts = cnt["g"]
        op("g", lambda g: g.dma_start(
            out=big0[:, 16384:32768].rearrange("p (k t) -> p k t", k=4),
            in_=xtsl.ap().rearrange("(k p) t -> p k t", p=128)), inc=("g", 16))
        ev_ub = cnt["g"]
        ev_alog = op("q", lambda q: q.dma_start(
            out=alog_s[:, :].rearrange("p (k n) -> p k n", k=4),
            in_=alog4.ap().rearrange("k p n -> p k n")),
            inc=("q", 16))
        op("q", lambda q: q.dma_start(
            out=bdt_s[:, :], in_=bdt4.ap().rearrange("k p -> p k"),
            allow_slow_non_contiguous=True), inc=("q", 16))
        op("q", lambda q: q.dma_start(
            out=dp_s[:, :], in_=dp4.ap().rearrange("k p -> p k"),
            allow_slow_non_contiguous=True), inc=("q", 16))
        ev_small_q = cnt["q"]
        ev_exp = op("s", lambda s: s.activation(negtmp[:, :], alog_s[:, :], AF.Exp),
                    waits=[("q", ev_alog)], inc=("s", 1))
        ev_mnega = op("v", lambda v: v.tensor_scalar_mul(alog_s[:, :], negtmp[:, :], -1.0),
                      waits=[("s", ev_exp)], inc=("v", 1))

        def _finish():
            tq = cnt["q"]
            tg = cnt["g"]
            op("q", lambda q: q.nop(), waits=[("q", tq)])
            op("g", lambda g: g.nop(), waits=[("g", tg)])

        dtu2w = lambda i: mid0[:, 512 * i: 512 * (i + 1)]
        b_w = mid0[:, 1024:1536]
        h_w = mid0[:, 1536:2048]
        hCw = lambda i: mid0[:, 2048 + 512 * i: 2048 + 512 * (i + 1)]
        yDw = lambda i: mid0[:, 3072 + 512 * i: 3072 + 512 * (i + 1)]
        ytmw = mid0[:, 4096:6144]
        xtbw = mid0[:, 8192:16384]
        Bcw = lambda n, par: big1[:, 8192 * par + 512 * n: 8192 * par + 512 * (n + 1)]
        Ccw = lambda n, par: big1[:, 16384 + 8192 * par + 512 * n:
                                  16384 + 8192 * par + 512 * (n + 1)]
        a_w = lambda i: fp0[:, 512 * i: 512 * (i + 1)]

        ev_xtb = [0] * 8
        ev_xz = [0] * 8
        ev_dtmm = [0] * 8
        ev_bc = [0] * 8
        ev_delta = [0] * 8
        ev_spfree = [[0] * 4 for _ in range(8)]
        ev_dtv = [[0] * 4 for _ in range(8)]
        ev_bcdw = [0] * 8
        ev_bcache = [0] * 8
        ev_a2aw = [0] * 8
        ev_hC_slot = [0, 0]
        ev_a_slot = [0, 0]
        ev_yD_slot = [0, 0]
        ev_ytr_slot = [0, 0]
        ev_py_free = [0, 0]
        ev_hC_last = [0] * 8
        ntr_ctr = [0]
        pyc = [0]

        def emit_1a(w):
            c0 = 512 * w
            wt = []
            if w >= 1:
                wt.append(("t", ev_xz[w - 1]))
            else:
                wt.append(("g", ev_g_consts))
            op("g", lambda g, c0=c0: g.dma_start(
                out=xtbw.rearrange("p (k t) -> p k t", k=16),
                in_=xt.ap()[:, :, c0:c0 + 512]), inc=("g", 16), waits=wt)
            ev_xtb[w] = cnt["g"]
            xzr = 512 * (w % 2)
            for k in range(16):
                wv = [("g", ev_xtb[w])]
                if k == 0 and w >= 2:
                    wv.append(("s", ev_bc[w - 2]))
                inc = ("t", 1) if k == 15 else None
                op("t", lambda t, k=k, xzr=xzr: t.matmul(
                    PB(xzr, xzr + 512)[0:96, :], wxb[:, 96 * k:96 * (k + 1)],
                    xtbw[:, 512 * k:512 * (k + 1)],
                    start=(k == 0), stop=(k == 15)), inc=inc, waits=wv)
            ev_xz[w] = cnt["t"]
            wv = [("t", ev_xz[w])]
            if w >= 2:
                wv.append(("t", ev_dtmm[w - 2]))
            op("s", lambda s, w=w, xzr=xzr: s.copy(
                deltab[:, 512 * (w % 2):512 * (w % 2) + 512],
                PB(xzr, xzr + 512)[0:64, :]), inc=("s", 1), waits=wv)
            ev_delta[w] = cnt["s"]
            op("s", lambda s, c0=c0, xzr=xzr: s.copy(
                bcrows[:, c0:c0 + 512], PB(xzr, xzr + 512)[64:96, :]), inc=("s", 1))
            ev_bc[w] = cnt["s"]
            for k in range(4):
                dtr = 1024 + 512 * (k % 2)
                wv = [("s", ev_delta[w])]
                if k > 1:
                    wv.append(("s", ev_spfree[w][k - 2]))
                elif w > 0:
                    wv.append(("s", ev_spfree[w - 1][2 + k]))
                ev_dt_k = op("t", lambda t, w=w, k=k, dtr=dtr: t.matmul(
                    PB(dtr, dtr + 512), wdtb[:, 128 * k:128 * (k + 1)],
                    deltab[:, 512 * (w % 2):512 * (w % 2) + 512],
                    start=True, stop=True), inc=("t", 1), waits=wv)
                if k == 3:
                    ev_dtmm[w] = ev_dt_k
                w2 = [("t", ev_dt_k)]
                if w == 0 and k == 0:
                    w2.append(("q", ev_small_q))
                if w > 0 or k > 0:
                    pj, pk = (w, k - 1) if k > 0 else (w - 1, 3)
                    w2.append(("v", ev_dtv[pj][pk]))
                op("s", lambda s, k=k, dtr=dtr: s.activation(
                    spb(0), PB(dtr, dtr + 512), AF.Abs, bias=bdt_s[:, k:k + 1]),
                    waits=w2)
                op("s", lambda s, k=k, dtr=dtr: s.activation(
                    spb(1), PB(dtr, dtr + 512), AF.Relu, bias=bdt_s[:, k:k + 1]),
                    inc=("s", 1))
                ev_spfree[w][k] = cnt["s"]
                op("s", lambda s: s.activation(spb(2), spb(0), AF.Exp, scale=-1.0))
                ev_spD = op("s", lambda s: s.activation(spb(0), spb(2), AF.Ln, bias=1.0),
                            inc=("s", 1))
                op("v", lambda v, k=k, c0=c0: v.tensor_tensor(
                    dtb(k)[:, c0:c0 + 512], spb(0), spb(1), ALU.add),
                    inc=("v", 1), waits=[("s", ev_spD)])
                ev_dtv[w][k] = cnt["v"]
            op("q", lambda q, w=w, c0=c0: q.dma_start(
                out=bcdB.ap()[w], in_=bcrows[0:16, c0:c0 + 512]),
                inc=("q", 16), waits=[("s", ev_bc[w])])
            op("q", lambda q, w=w, c0=c0: q.dma_start(
                out=bcdC.ap()[w], in_=bcrows[16:32, c0:c0 + 512]),
                inc=("q", 16))
            ev_bcdw[w] = cnt["q"]

        def emit_bcache(w):
            par = w % 2
            c0 = 512 * w
            wv = [("q", ev_bcdw[w])]
            if w >= 2:
                wv.append(("v", ev_hC_last[w - 2]))
            op("g", lambda g, par=par, w=w: g.dma_start(
                out=big1[:, 8192 * par: 8192 * par + 8192].rearrange(
                    "p (n t) -> p n t", n=16),
                in_=AP(bcdB.ap().tensor, 8192 * w, [[0, 128], [512, 16], [1, 512]])),
                inc=("g", 16), waits=wv)
            op("g", lambda g, par=par, w=w: g.dma_start(
                out=big1[:, 16384 + 8192 * par: 16384 + 8192 * par + 8192].rearrange(
                    "p (n t) -> p n t", n=16),
                in_=AP(bcdC.ap().tensor, 8192 * w, [[0, 128], [512, 16], [1, 512]])),
                inc=("g", 16))
            ev_bcache[w] = cnt["g"]

        def emit_1b(w):
            par = w % 2
            c0 = 512 * w
            for d in range(4):
                wv = [("g", ev_ub), ("v", ev_dtv[w][d])]
                op("v", lambda v, d=d, c0=c0: v.tensor_mul(
                    dtu2w(d % 2), dtb(d)[:, c0:c0 + 512], ub(d)[:, c0:c0 + 512]),
                    waits=wv)
                for n in range(16):
                    ia = n % 2
                    wv = [("v", max(ev_a_slot[ia], ev_mnega, ev_dtv[w][d]))]
                    ev_a = op("s", lambda s, d=d, n=n, c0=c0, ia=ia: s.activation(
                        a_w(ia), dtb(d)[:, c0:c0 + 512], AF.Exp,
                        scale=alog_s[:, 16 * d + n:16 * d + n + 1]),
                        inc=("s", 1), waits=wv)
                    op("v", lambda v, n=n, d=d, par=par: v.tensor_mul(
                        b_w, dtu2w(d % 2), Bcw(n, par)),
                        waits=[("g", ev_bcache[w])])
                    init = 0.0 if w == 0 else carry[:, 16 * d + n:16 * d + n + 1]
                    op("v", lambda v, ia=ia, init=init: v.tensor_tensor_scan(
                        h_w, a_w(ia), b_w, init, ALU.mult, ALU.add),
                        inc=("v", 1), waits=[("s", ev_a)])
                    ev_a_slot[ia] = cnt["v"]
                    ev_scan = cnt["v"]
                    if w < 7:
                        op("s", lambda s, d=d, n=n: s.copy(
                            carry[:, 16 * d + n:16 * d + n + 1], h_w[:, 511:512]),
                            waits=[("v", ev_scan)])
                    ic = n % 2
                    op("v", lambda v, n=n, ic=ic, par=par: v.tensor_mul(
                        hCw(ic), h_w, Ccw(n, par)), waits=[("t", ev_hC_slot[ic])])
                    ev_hC = op("v", lambda v: v.tensor_copy(
                        vdum[:, 8:16], vdum[:, 0:8]), inc=("v", 1))
                    pyr = 2048 + 512 * (pyc[0] % 2)
                    wv = [("v", ev_hC)]
                    if n == 0:
                        if w == 0 and d == 0:
                            wv.append(("s", S_P1A))
                        pv = ev_py_free[pyc[0] % 2]
                        if pv:
                            wv.append(("v", pv))
                    op("t", lambda t, pyr=pyr, ic=ic, n=n: t.matmul(
                        PB(pyr, pyr + 512), identb[:, :], hCw(ic),
                        start=(n == 0), stop=(n == 15)), inc=("t", 1), waits=wv)
                    ev_hC_slot[ic] = cnt["t"]
                ev_hC_last[w] = cnt["v"]
                ev_ymm = cnt["t"]
                iy = d % 2
                pyr = 2048 + 512 * (pyc[0] % 2)
                wv = [("t", ev_ymm), ("t", ev_yD_slot[iy]), ("q", ev_small_q)]
                op("v", lambda v, d=d, c0=c0, pyr=pyr, iy=iy:
                    v.scalar_tensor_tensor(
                        yDw(iy), ub(d)[:, c0:c0 + 512], dp_s[:, d:d + 1],
                        PB(pyr, pyr + 512), ALU.mult, ALU.add), waits=wv)
                ev_stt = op("v", lambda v: v.tensor_copy(
                    vdum[:, 8:16], vdum[:, 0:8]), inc=("v", 1))
                ev_py_free[pyc[0] % 2] = ev_stt
                pyc[0] += 1
                for ts in range(4):
                    m = ntr_ctr[0] % 2
                    ntr_ctr[0] += 1
                    wv = [("v", ev_stt), ("s", ev_ytr_slot[m])]
                    ev_ytr = op("t", lambda t, iy=iy, ts=ts, m=m: t.matmul(
                        PB(3072 + 512 * m, 3200 + 512 * m),
                        yDw(iy)[:, 128 * ts:128 * (ts + 1)], identb[:, :],
                        start=True, stop=True), inc=("t", 1), waits=wv)
                    ev_yD_slot[iy] = ev_ytr
                    wv = [("t", ev_ytr)]
                    if w >= 1 and d == 0 and ts == 0:
                        wv.append(("q", ev_a2aw[w - 1]))
                    op("s", lambda s, ts=ts, d=d, m=m: s.copy(
                        ytmw[:, 512 * ts + 128 * d: 512 * ts + 128 * (d + 1)],
                        PB(3072 + 512 * m, 3200 + 512 * m)), inc=("s", 1), waits=wv)
                    ev_ytr_slot[m] = cnt["s"]
            ev_ytm_done = cnt["s"]
            op("q", lambda q, c0=c0: q.dma_start(
                out=a2ai.ap()[c0:c0 + 512, :].rearrange("(ts p) d -> p ts d", p=128),
                in_=ytmw[:, :].rearrange("p (ts d) -> p ts d", ts=4)),
                inc=("q", 16), waits=[("s", ev_ytm_done)])
            ev_a2aw[w] = cnt["q"]

        S_P1A = 0
        for step in range(9):
            if step < 8:
                emit_1a(step)
                if step == 0:
                    S_P1A = cnt["s"]
                emit_bcache(step)
            if step >= 1:
                emit_1b(step - 1)
        S_P1B = cnt["s"]
        if stop <= 2:
            return _tail(nc, ops, cnt, sems, block, _finish)

        op("g", lambda g: g.collective_compute(
            "AllToAll", ALU.bypass, replica_groups=[list(range(8))],
            ins=[a2ai.ap().opt()], outs=[a2ao.ap().opt()]),
            inc=("c", 1), waits=[("q", ev_a2aw[7])])
        EV_CC = op("g", lambda g: g.collective_compute(
            "AllReduce", ALU.add, replica_groups=[list(range(8))],
            ins=[barin.ap().opt()], outs=[barout.ap().opt()]),
            inc=("c", 1))

        if stop <= 3:
            op("g", lambda g: g.nop(), waits=[("c", EV_CC)])
            return _tail(nc, ops, cnt, sems, block, _finish)
        if "dump" in DBG:
            op("q", lambda q: q.dma_start(out=dbgi.ap(), in_=a2ai.ap()),
               inc=("q", 16), waits=[("c", EV_CC)])
            op("q", lambda q: q.dma_start(out=dbgo.ap(), in_=a2ao.ap()),
               inc=("q", 16))
        lr_slotA_s = 0
        lr_slotB_s = 0
        lr_slotC_q = 0
        lr_ytile_v = [0, 0]
        lr_zbf_t = 0
        ztr_slot_s = [0] * 4
        ev_gatemm = {}
        ev_linmm = {}
        ev_silS = {}
        ev_gatV = {}
        ev_h128 = {}
        ev_hevac = {}
        ev_region_free = {}
        ev_down_last = 0
        ev_fold = 0
        ev_grp_end_v = 0
        for grp in range(2):
            ev_z32dma = [0] * 4
            ev_zT_written = 0
            for tt in range(4):
                it = tt % 2
                w = [("c", EV_CC), ("s", lr_slotA_s), ("v", lr_ytile_v[it])]
                op("q", lambda q, grp=grp, tt=tt: q.dma_start(
                    out=slotA, in_=xres.ap()[grp, 128 * tt:128 * (tt + 1), :]),
                    inc=("q", 16), waits=w)
                for q2 in range(4):
                    r = 4 * grp + q2
                    op("q", lambda q, r=r, tt=tt, q2=q2, it=it: q.dma_start(
                        out=ytile(it)[:, 512 * q2:512 * (q2 + 1)],
                        in_=a2ao.ap()[512 * r + 128 * tt: 512 * r + 128 * (tt + 1), :]),
                        inc=("q", 16))
                ev_yt = cnt["q"]
                w = [("q", ev_yt), ("s", lr_slotB_s)]
                ev_resid = op("v", lambda v, it=it: v.scalar_tensor_tensor(
                    slotB, slotA, 1.0, ytile(it), ALU.mult, ALU.add,
                    accum_out=ln_s[:, 0:1]), inc=("v", 1), waits=w)
                lr_ytile_v[it] = ev_resid
                ev_rs1 = ev_resid
                ev_ng = op("s", lambda s: s.mul(ln_s[:, 1:2], ln_s[:, 0:1], -INV_D),
                           inc=("s", 1), waits=[("v", ev_rs1)])
                lr_slotB_s = op("s", lambda s: s.activation(
                    slotA, slotB, AF.Identity, bias=ln_s[:, 1:2]), inc=("s", 1),
                    waits=[("s", ev_ng)])
                op("s", lambda s: s.activation(
                    slotC, slotA, AF.Square, accum_out=ln_s[:, 2:3]),
                    inc=("s", 1), waits=[("q", lr_slotC_q)])
                op("s", lambda s: s.activation(
                    ln_s[:, 3:4], ln_s[:, 2:3], AF.Identity, bias=eps_t[:, :],
                    scale=INV_D), waits=[("g", 16)])
                ev_sd = op("s", lambda s: s.activation(
                    ln_s[:, 4:5], ln_s[:, 3:4], AF.Sqrt), inc=("s", 1))
                ev_rstd = op("v", lambda v: v.reciprocal(ln_s[:, 5:6], ln_s[:, 4:5]),
                             inc=("v", 1), waits=[("s", ev_sd)])
                w = [("v", ev_rstd), ("t", lr_zbf_t)]
                ev_zbf = op("s", lambda s: s.activation(
                    zbf, slotA, AF.Copy, scale=ln_s[:, 5:6]), inc=("s", 1), waits=w)
                ev_z32 = op("s", lambda s: s.activation(
                    slotC, slotA, AF.Copy, scale=ln_s[:, 5:6]), inc=("s", 1))
                lr_slotA_s = ev_z32
                op("q", lambda q, grp=grp, tt=tt: q.dma_start(
                    out=z32d.ap()[grp, 128 * tt:128 * (tt + 1), :], in_=slotC),
                    inc=("q", 16), waits=[("s", ev_z32)])
                ev_z32dma[tt] = cnt["q"]
                lr_slotC_q = cnt["q"]
                for k in range(16):
                    m = k % 4
                    w = [("s", ev_zbf), ("s", ztr_slot_s[m])]
                    if grp == 0 and tt == 0 and k == 0:
                        w.append(("s", S_P1B))
                    if grp == 1 and tt == 0 and k < 4:
                        pv = ev_region_free.get(4 + m)
                        if pv:
                            w.append(("v", pv))
                    ev_ztr = op("t", lambda t, k=k, m=m: t.matmul(
                        PB(2048 + 512 * m, 2176 + 512 * m),
                        zbf[:, 128 * k:128 * (k + 1)], identb[:, :],
                        start=True, stop=True), inc=("t", 1), waits=w)
                    lr_zbf_t = ev_ztr
                    ztr_slot_s[m] = op("s", lambda s, k=k, tt=tt, m=m: s.copy(
                        zTv(k)[:, 128 * tt:128 * (tt + 1)],
                        PB(2048 + 512 * m, 2176 + 512 * m)),
                        inc=("s", 1), waits=[("t", ev_ztr)])
            ev_zT_written = cnt["s"]

            def emit_fold_blocks(i2):
                for st_ in range(6):
                    blksz = 2 << st_
                    if (i2 + 1) % blksz != 0:
                        break
                    m = i2 // blksz
                    inner = 512 << st_
                    base = m * blksz * 512
                    first_w = [("s", ev_hevac[64 * grp + i2])] if st_ == 0 else []
                    sub = min(inner, 8192)
                    for off in range(0, inner, sub):
                        A_ = big0[:, base + off: base + off + sub]
                        B_ = big0[:, base + inner + off: base + inner + off + sub]
                        S_ = big1[:, 0:sub]
                        op("v", lambda v, A_=A_, S_=S_: v.tensor_copy(S_, A_),
                           waits=first_w)
                        first_w = []
                        op("v", lambda v, A_=A_, B_=B_, S_=S_: v.tensor_tensor(
                            A_, S_, B_, ALU.add))
                        inc = ("v", 1) if (st_ == 5 and off + sub == inner) else None
                        op("v", lambda v, B_=B_, S_=S_: v.tensor_tensor(
                            B_, S_, B_, ALU.subtract), inc=inc)

            def emit_h128(iu2, i2):
                w2 = [("v", ev_gatV[iu2])]
                if iu2 >= 2:
                    w2.append(("s", ev_hevac[iu2 - 2]))
                ev_h128[iu2] = op("t", lambda t, i2=i2: t.matmul(
                    PB(1024 + 512 * (i2 % 2), 1536 + 512 * (i2 % 2)),
                    hhb[:, :], gatb(i2 % 2),
                    start=True, stop=True), inc=("t", 1), waits=w2)
                w2 = [("t", ev_h128[iu2])]
                if grp == 1 and i2 == 0:
                    w2.append(("t", ev_down_last))
                    w2.append(("v", ev_fold))
                ev_hevac[iu2] = op("s", lambda s, i2=i2: s.copy(
                    hidA(i2), PB(1024 + 512 * (i2 % 2), 1536 + 512 * (i2 % 2))),
                    inc=("s", 1), waits=w2)
                emit_fold_blocks(i2)

            for i in range(64):
                iu = 64 * grp + i
                w = []
                if iu >= 2:
                    w.append(("t", ev_gatemm[iu - 2]))
                else:
                    w.append(("v", ev_hC_last[7]))
                ev_w1 = op("g", lambda g, i=i: g.dma_start(
                    out=w1b(i % 2).rearrange("p (k h) -> p k h", k=16),
                    in_=w1t.ap()[:, i, :, :]), inc=("g", 16), waits=w)
                w = []
                if iu >= 2:
                    w.append(("t", ev_linmm[iu - 2]))
                else:
                    w.append(("v", ev_hC_last[7]))
                ev_w3 = op("g", lambda g, i=i: g.dma_start(
                    out=w3b(i % 2).rearrange("p (k h) -> p k h", k=16),
                    in_=w3t.ap()[:, i, :, :]), inc=("g", 16), waits=w)
                if i >= 1:
                    emit_h128(iu - 1, i - 1)
                gr = 2048 * (i % 2)
                lr = 512 + 2048 * (i % 2)
                for k in range(16):
                    w = [("g", ev_w1), ("s", ev_zT_written)]
                    if k == 0 and iu >= 2:
                        w.append(("s", ev_silS[iu - 2]))
                    inc = ("t", 1) if k == 15 else None
                    op("t", lambda t, i=i, k=k, gr=gr: t.matmul(
                        PB(gr, gr + 512), w1b(i % 2)[:, 128 * k:128 * (k + 1)], zTv(k),
                        start=(k == 0), stop=(k == 15)), inc=inc, waits=w)
                ev_gatemm[iu] = cnt["t"]
                for k in range(16):
                    w = [("g", ev_w3)]
                    if k == 0 and iu >= 2:
                        w.append(("v", ev_gatV[iu - 2]))
                    inc = ("t", 1) if k == 15 else None
                    op("t", lambda t, i=i, k=k, lr=lr: t.matmul(
                        PB(lr, lr + 512), w3b(i % 2)[:, 128 * k:128 * (k + 1)], zTv(k),
                        start=(k == 0), stop=(k == 15)), inc=inc, waits=w)
                ev_linmm[iu] = cnt["t"]
                w = [("t", ev_gatemm[iu])]
                if iu >= 2:
                    w.append(("v", ev_gatV[iu - 2]))
                ev_silS[iu] = op("s", lambda s, i=i, gr=gr: s.activation(
                    silb(i % 2), PB(gr, gr + 512), AF.Silu), inc=("s", 1), waits=w)
                w = [("s", ev_silS[iu]), ("t", ev_linmm[iu])]
                if iu >= 2:
                    w.append(("t", ev_h128[iu - 2]))
                op("v", lambda v, i=i, lr=lr: v.scalar_tensor_tensor(
                    gatb(i % 2), PB(lr, lr + 512), S8192, silb(i % 2),
                    ALU.mult, ALU.mult), waits=w)
                ev_gatV[iu] = op("v", lambda v: v.tensor_copy(
                    vdum[:, 8:16], vdum[:, 0:8]), inc=("v", 1))

                if i == 63:
                    emit_h128(iu, i)
            ev_up_done = cnt["s"]

            ev_fold = cnt["v"]

            ev_w2 = {}
            ev_dmm = {}
            ev_w2used = {}
            for sb_ in range(8):
                for hf in range(2):
                    w = []
                    if sb_ == 0 and hf == 1:
                        w.append(("t", ev_xz[7]))
                        w.append(("q", ev_a2aw[7]))
                    elif sb_ == 0:
                        w.append(("t", ev_linmm[64 * grp + 63]))
                        w.append(("q", ev_a2aw[7]))
                    else:
                        w.append(("t", ev_w2used[(sb_ - 1, hf)]))
                    ev_w2[(sb_, hf)] = op("g", lambda g, sb_=sb_, hf=hf: g.dma_start(
                        out=w2b(hf).rearrange("p (k d) -> p k d", k=4),
                        in_=w2t.ap()[:, 8 * sb_ + 4 * hf:8 * sb_ + 4 * hf + 4, :]),
                        inc=("g", 16), waits=w)
                for blk in range(2):
                    for hf in range(2):
                        for hk in range(4):
                            i = 8 * sb_ + 4 * hf + hk
                            st2 = (hf == 0 and hk == 0)
                            sp2 = (hf == 1 and hk == 3)
                            for ts in (2 * blk, 2 * blk + 1):
                                for ds in range(4):
                                    ridx = 4 * (ts % 2) + ds
                                    reg = 512 * ridx
                                    w = [("g", ev_w2[(sb_, hf)])]
                                    if st2 and ds == 0:
                                        w.append(("v", ev_fold))
                                    if st2:
                                        pv = ev_region_free.get(ridx)
                                        if pv:
                                            w.append(("v", pv))
                                        if grp == 0 and sb_ == 0 and blk == 0:
                                            w.append(("s", ev_zT_written))
                                    last_of_hfblk = (hk == 3 and ds == 3
                                                     and ts == 2 * blk + 1)
                                    inc = ("t", 1) if ((sp2 and ds == 3)
                                                       or (blk == 1 and last_of_hfblk)) else None
                                    op("t", lambda t, i=i, ts=ts, ds=ds, reg=reg,
                                        st2=st2, sp2=sp2, hf=hf, hk=hk: t.matmul(
                                        PB(reg, reg + 512),
                                        hidA(i)[:, 128 * ts:128 * ts + 128],
                                        w2b(hf)[:, 2048 * hk + 512 * ds:
                                                2048 * hk + 512 * ds + 512],
                                        start=st2, stop=sp2), inc=inc, waits=w)
                                    if sp2 and ds == 3:
                                        ev_dmm[(sb_, ts)] = cnt["t"]
                                    if blk == 1 and last_of_hfblk:
                                        ev_w2used[(sb_, hf)] = cnt["t"]
                    for ts in (2 * blk, 2 * blk + 1):
                        for ds in range(4):
                            ridx = 4 * (ts % 2) + ds
                            reg = 512 * ridx
                            w = [("t", ev_dmm[(sb_, ts)])]
                            if sb_ == 0:
                                op("v", lambda v, ts=ts, ds=ds, reg=reg: v.tensor_copy(
                                    accb(ts)[:, 512 * ds:512 * (ds + 1)],
                                    PB(reg, reg + 512)), inc=("v", 1), waits=w)
                            else:
                                op("v", lambda v, ts=ts, ds=ds, reg=reg: v.tensor_tensor(
                                    accb(ts)[:, 512 * ds:512 * (ds + 1)],
                                    PB(reg, reg + 512),
                                    accb(ts)[:, 512 * ds:512 * (ds + 1)],
                                    ALU.add), inc=("v", 1), waits=w)
                            ev_region_free[ridx] = cnt["v"]
            ev_down_last = ev_dmm[(7, 3)]

            ev_outdma = 0
            ev_o_prev = 0
            for ts in range(4):
                w = [("q", ev_z32dma[ts]), ("s", max(ev_o_prev, lr_slotA_s))]
                op("q", lambda q, grp=grp, ts=ts: q.dma_start(
                    out=slotA, in_=z32d.ap()[grp, 128 * ts:128 * (ts + 1), :]),
                    inc=("q", 16), waits=w)
                ev_zr = cnt["q"]
                w = [("q", ev_zr), ("v", ev_region_free[4 * (ts % 2) + 3]),
                     ("s", lr_slotB_s)]
                ev_r2 = op("v", lambda v, ts=ts: v.scalar_tensor_tensor(
                    slotB, slotA, 1.0, accb(ts), ALU.mult, ALU.add,
                    accum_out=ln_s[:, 0:1]), inc=("v", 1), waits=w)
                ev_rs1b = ev_r2
                ev_ngb = op("s", lambda s: s.mul(ln_s[:, 1:2], ln_s[:, 0:1], -INV_D),
                            inc=("s", 1), waits=[("v", ev_rs1b)])
                lr_slotB_s = op("s", lambda s: s.activation(
                    slotA, slotB, AF.Identity, bias=ln_s[:, 1:2]), inc=("s", 1),
                    waits=[("s", ev_ngb)])
                op("s", lambda s: s.activation(
                    slotC, slotA, AF.Square, accum_out=ln_s[:, 2:3]),
                    inc=("s", 1), waits=[("q", lr_slotC_q)])
                op("s", lambda s: s.activation(
                    ln_s[:, 3:4], ln_s[:, 2:3], AF.Identity, bias=eps_t[:, :],
                    scale=INV_D), waits=[("g", 16)])
                ev_sd2 = op("s", lambda s: s.activation(
                    ln_s[:, 4:5], ln_s[:, 3:4], AF.Sqrt), inc=("s", 1))
                ev_rstd2 = op("v", lambda v: v.reciprocal(ln_s[:, 5:6], ln_s[:, 4:5]),
                              inc=("v", 1), waits=[("s", ev_sd2)])
                ev_o = op("s", lambda s: s.activation(
                    slotC, slotA, AF.Copy, scale=ln_s[:, 5:6]),
                    inc=("s", 1), waits=[("v", ev_rstd2)])
                ev_o_prev = ev_o
                lr_slotA_s = ev_o
                op("q", lambda q, grp=grp, ts=ts: q.dma_start(
                    out=out.ap()[grp, 128 * ts:128 * (ts + 1), :], in_=slotC),
                    inc=("q", 16), waits=[("s", ev_o)])
                ev_outdma = cnt["q"]
                lr_slotC_q = ev_outdma
            ev_grp_end_v = cnt["v"]

        if "dump" in DBG:
            vq = cnt["q"]
            op("q", lambda q: q.dma_start(out=dbgz.ap(), in_=z32d.ap()),
               inc=("q", 16), waits=[("q", vq)])
        total_q = cnt["q"]
        op("q", lambda q: q.nop(), waits=[("q", total_q)])

        streams = {"t": [], "s": [], "v": [], "g": [], "q": []}
        for e in ops:
            streams[e[0]].append(e)

        def make(eng_ops):
            def f(h):
                last = {}
                for _, fn, inc, waits in eng_ops:
                    for sm, vl in waits:
                        if last.get(sm, 0) < vl:
                            h.wait_ge(sems[sm], vl)
                            last[sm] = vl
                    ins = fn(h)
                    if inc is not None:
                        ins.then_inc(sems[inc[0]], inc[1])
            return f

        block.tensor(make(streams["t"]))
        block.scalar(make(streams["s"]))
        block.vector(make(streams["v"]))
        block.gpsimd(make(streams["g"]))
        block.sync(make(streams["q"]))

    return nc


def kernel(L0_attn_x, L0_attn_Wx, L0_attn_Wdt, L0_attn_bdt, L0_attn_A_log,
           L0_attn_Dp, L0_ffn_W1, L0_ffn_W2, L0_ffn_W3):
    x = np.ascontiguousarray(np.asarray(L0_attn_x, dtype=np.float32))
    Wx = np.ascontiguousarray(np.asarray(L0_attn_Wx, dtype=np.float32))
    Wdt = np.asarray(L0_attn_Wdt, dtype=np.float32)
    bdt = np.asarray(L0_attn_bdt, dtype=np.float32)
    A_log = np.asarray(L0_attn_A_log, dtype=np.float32)
    Dp = np.asarray(L0_attn_Dp, dtype=np.float32)
    w1t = np.ascontiguousarray(np.asarray(L0_ffn_W1, dtype=np.float32).T)
    w2t = np.ascontiguousarray(np.asarray(L0_ffn_W2, dtype=np.float32).T)
    w3t = np.ascontiguousarray(np.asarray(L0_ffn_W3, dtype=np.float32).T)
    hh = _hadamard(128)
    idm = np.eye(128, dtype=np.float32)

    xT = [np.ascontiguousarray(x[b].T) for b in range(B)]
    xts = [np.ascontiguousarray(xT[b].reshape(16, 128, S).transpose(1, 0, 2))
           for b in range(B)]
    w1s = np.ascontiguousarray(w1t.reshape(16, 128, 64, 128).transpose(1, 2, 0, 3))
    w3s = np.ascontiguousarray(w3t.reshape(16, 128, 64, 128).transpose(1, 2, 0, 3))
    w2s = np.ascontiguousarray(w2t.reshape(64, 128, D).transpose(1, 0, 2))

    in_maps = []
    tokmap = [np.arange(TEI * c, TEI * (c + 1)) for c in range(8)]
    for c in range(8):
        b, sl = c // 4, c % 4
        dlo = DSL * sl
        in_maps.append({
            "xt": xts[b],
            "xtsl": np.ascontiguousarray(xT[b][dlo:dlo + DSL]),
            "xres": np.ascontiguousarray(x[:, tokmap[c], :]),
            "wx": Wx,
            "wdt": np.ascontiguousarray(Wdt[:, dlo:dlo + DSL]),
            "bdt4": np.ascontiguousarray(bdt[dlo:dlo + DSL].reshape(4, 128)),
            "alog4": np.ascontiguousarray(A_log[dlo:dlo + DSL].reshape(4, 128, NST)),
            "dp4": np.ascontiguousarray(Dp[dlo:dlo + DSL].reshape(4, 128)),
            "w1t": w1s, "w3t": w3s, "w2t": w2s,
            "hhm": hh, "idm": idm,
        })

    nc = build_nc()
    res = run_bass_kernel_spmd(nc, in_maps, core_ids=list(range(8)))
    out_full = np.empty((B, S, D), dtype=np.float32)
    for c in range(8):
        out_full[:, tokmap[c], :] = res.results[c]["out"]
    return out_full
